# version 12
# baseline (speedup 1.0000x reference)
import numpy as np
from contextlib import ExitStack

import concourse.bass as bass
import concourse.mybir as mybir
from concourse.bass_utils import run_bass_kernel_spmd

F32 = mybir.dt.float32
BF16 = mybir.dt.bfloat16
I32 = mybir.dt.int32
AF = mybir.ActivationFunctionType
ALU = mybir.AluOpType
AX = mybir.AxisListType.X

NCORES = 8
TOK = 1024
NT = TOK // 128
D = 4096
KD = D // 128
EPS = 1e-5

H, Pd, Nd, G, Kc, CH = 128, 64, 128, 8, 4, 128
INTER = H * Pd
CONV_DIM = INTER + 2 * G * Nd
PROJ = INTER + CONV_DIM + H
NCHUNK = TOK // CH


class P:

    def __init__(self, name):
        self.nc = bass.Bass("TRN2", target_bir_lowering=False, name=name)
        self.gstack = ExitStack()
        self._stack = self.gstack
        self.prefix = ""
        self.tot = {}
        self._semn = 0
        self._bar = None

    @property
    def stack(self):
        return self._stack

    def push(self, name):
        self.prefix = name + "_"
        self._stack = ExitStack()
        self._cc = {}
        return self._stack

    def barrier(self):
        nc = self.nc
        if self._bar is None:
            self.prefix, sv = "", self.prefix
            self._bar = self.sem("stagebar")
            self.prefix = sv
        engs = [nc.tensor, nc.vector, nc.scalar, nc.gpsimd, nc.sync]
        for e in engs:
            self.inc(e.drain(), self._bar)
        for e in engs:
            self.wait(e, self._bar)

    def sem(self, name=None):
        name = name or f"s{self._semn}"
        self._semn += 1
        if not hasattr(self, "_sems"):
            self._sems = {}
        if name not in self._sems:
            s = self.gstack.enter_context(self.nc.semaphore(name))
            self.tot[s.name] = 0
            self._sems[name] = s
        return self._sems[name]

    def inc(self, instr, sem, amt=1):
        instr.then_inc(sem, amt)
        self.tot[sem.name] += amt
        return instr

    def incd(self, eng, sem, amt=1):
        return self.inc(eng.drain(), sem, amt)

    def dma(self, q, dst, src, sem, **kw):
        ins = q.dma_start(dst, src, **kw)
        return self.inc(ins, sem, 16)

    def wait(self, eng, sem, slack=0):
        eng.wait_ge(sem, self.tot[sem.name] - slack)

    def sb(self, name, shape, dtype):
        return self._stack.enter_context(self.nc.sbuf_tensor(self.prefix + name, shape, dtype))

    def ps(self, name, shape, dtype=F32):
        return self._stack.enter_context(self.nc.psum_tensor(self.prefix + name, shape, dtype))

    def ccol(self, val):
        if not hasattr(self, "_cc"):
            self._cc = {}
        if not hasattr(self, "cst"):
            pre, self.prefix = self.prefix, ""
            self.cst = self.sem("cst")
            self.prefix = pre
        if val not in self._cc:
            t = self.sb(f"cc{len(self._cc)}", [128, 1], F32)
            self.inc(self.nc.gpsimd.memset(t[:, :], float(val)), self.cst)
            self._cc[val] = t
        return self._cc[val][:, 0:1]

    def wait_consts(self, *engs):
        if hasattr(self, "cst"):
            for e in engs:
                self.wait(e, self.cst)

    def make_identity(self, dtype=BF16):
        nc = self.nc
        ident = self.sb("identity", [128, 128], dtype)
        s = self.sem("ident")
        nc.gpsimd.memset(ident[:, :], 0.0)
        self.inc(nc.gpsimd.affine_select(ident[:, :], ident[:, :], [[-1, 128]],
                                         ALU.not_equal, 1.0, channel_multiplier=1), s)
        return ident, s


def ln_tile(p, x_f32, out_bf, scratch, s1, s2, ssq, svar, srstd, sdone):
    nc = p.nc
    nf = x_f32.shape[1]
    nc.vector.reduce_sum(s1[:, 0:1], x_f32[:, :], axis=AX)
    nc.vector.tensor_tensor(scratch[:, :], x_f32[:, :], x_f32[:, :], ALU.mult)
    nc.vector.reduce_sum(s1[:, 1:2], scratch[:, :], axis=AX)
    p.incd(nc.vector, ssq)
    nc.vector.drain()
    nc.vector.tensor_scalar(s2[:, 0:1], s1[:, 0:1], 1.0 / nf, None, ALU.mult)
    nc.vector.tensor_scalar(s2[:, 1:2], s1[:, 1:2], 1.0 / nf, None, ALU.mult)
    nc.vector.drain()
    nc.vector.tensor_tensor(s1[:, 0:1], s2[:, 0:1], s2[:, 0:1], ALU.mult)
    nc.vector.drain()
    nc.vector.tensor_sub(s2[:, 1:2], s2[:, 1:2], s1[:, 0:1])
    p.incd(nc.vector, svar)
    p.wait(nc.scalar, svar)
    b_eps, s_half = p.ccol(EPS), p.ccol(-0.5)
    p.wait_consts(nc.scalar)
    nc.scalar.activation(s1[:, 1:2], s2[:, 1:2], AF.Ln, bias=b_eps)
    nc.scalar.drain()
    nc.scalar.activation(s2[:, 1:2], s1[:, 1:2], AF.Exp, scale=s_half)
    p.incd(nc.scalar, srstd)
    p.wait(nc.vector, srstd)
    p.inc(nc.vector.tensor_scalar(out_bf[:, :], x_f32[:, :], s2[:, 0:1], s2[:, 1:2],
                                  ALU.subtract, ALU.mult), sdone)


def ln_and_transpose(p, x_in, xt, ident, ident_s, ntiles=NT):
    nc = p.nc
    nf = len(xt) * 128
    xrow = p.sb("ln_xrow", [128, nf], F32)
    p.ln_xrow = xrow
    xln = p.sb("ln_xln", [128, nf], BF16)
    scr = p.sb("ln_scr", [128, nf], F32)
    p.ln_scr = scr
    p.ln_xln = xln
    st1 = p.sb("ln_st1", [128, 2], F32)
    st2 = p.sb("ln_st2", [128, 2], F32)
    ld = p.sem("ln_ld")
    ssq, svar, srstd, sdone = (p.sem("ln_ssq"), p.sem("ln_svar"),
                               p.sem("ln_srstd"), p.sem("ln_sdone"))
    tr = p.sem("ln_tr")
    xtdone = p.sem("ln_xt")
    p.wait(nc.tensor, ident_s)
    it = 0
    for t in range(ntiles):
        p.wait(nc.sync, sdone)
        p.wait(nc.sync, ssq)
        p.dma(nc.sync, xrow[:, :], x_in[t * 128:(t + 1) * 128, :], ld)
        p.wait(nc.vector, ld)
        p.wait(nc.scalar, ld)
        ln_tile(p, xrow, xln, scr, st1, st2, ssq, svar, srstd, sdone)
        p.wait(nc.tensor, sdone)
        for k in range(len(xt)):
            psl = p.pst_ln[it % 2][:, :]
            p.wait(nc.tensor, xtdone, slack=0 if it < 2 else 1)
            p.inc(nc.tensor.transpose(psl, xln[:, k * 128:(k + 1) * 128],
                                      ident[:, :]), tr)
            p.wait(nc.vector, tr)
            p.inc(nc.vector.tensor_copy(xt[k][:, t * 128:(t + 1) * 128], psl), xtdone)
            it += 1
    return xtdone


def load_and_transpose(p, x_in, xt, ident, ident_s, scale_sb=None, ntiles=NT):
    nc = p.nc
    nf = len(xt) * 128
    xrow = p.sb("lt_xrow", [128, nf], BF16)
    ld = p.sem("lt_ld")
    tr = p.sem("lt_tr")
    xtdone = p.sem("lt_xt")
    p.wait(nc.tensor, ident_s)
    it = 0
    for t in range(ntiles):
        p.wait(nc.sync, xtdone)
        p.dma(nc.sync, xrow[:, :], x_in[t * 128:(t + 1) * 128, :], ld)
        p.wait(nc.tensor, ld)
        for k in range(len(xt)):
            psl = p.pst_ln[it % 2][:, :]
            it += 1
            p.wait(nc.tensor, xtdone, slack=0 if it <= 2 else 1)
            p.inc(nc.tensor.transpose(psl, xrow[:, k * 128:(k + 1) * 128],
                                      ident[:, :]), tr)
            p.wait(nc.vector, tr)
            if scale_sb is not None:
                ins = nc.vector.tensor_scalar(xt[k][:, t * 128:(t + 1) * 128],
                                              psl, scale_sb[:, k:k + 1], None, ALU.mult)
            else:
                ins = nc.vector.tensor_copy(xt[k][:, t * 128:(t + 1) * 128], psl)
            p.inc(ins, xtdone)
    return xtdone


def emit_A1(p, x_in, w_in, gate_o, hbc_o, dt_o, xh_in=None):
    nc = p.nc
    hof = 3 if xh_in is not None else 0
    with p.stack:
        ident, ident_s = p.make_identity(BF16)
        p.pst_ln = [p.ps("pst_ln0", [128, 128], BF16), p.ps("pst_ln1", [128, 128], BF16)]
        xt = [p.sb(f"xt{k}", [128, TOK], BF16) for k in range(KD)]
        xtdone = ln_and_transpose(p, x_in, xt, ident, ident_s)
        if xh_in is not None:
            xhr = p.ln_xrow
            xhl = p.ln_xln
            hscr = p.ln_scr
            hs1 = p.sb("hs1", [128, 2], F32)
            hs2 = p.sb("hs2", [128, 2], F32)
            xthb = p.sb("xthb", [128, KD * 4], BF16)
            hld_ = p.sem("hld_")
            hssq, hsvar, hsrstd, hsdone = (p.sem("h_ssq"), p.sem("h_svar"),
                                           p.sem("h_srstd"), p.sem("h_sdone"))
            htr = p.sem("htr")
            hxt = p.sem("hxt")
            p.wait(nc.vector, xtdone)
            nc.vector.memset(xhr[:, :], 0.0)
            p.incd(nc.vector, hld_)
            p.wait(nc.sync, hld_)
            p.dma(nc.sync, xhr[0:3, :].rearrange("a b -> a b"), xh_in[:, :], hld_)
            p.wait(nc.vector, hld_)
            p.wait(nc.scalar, hld_)
            ln_tile(p, xhr, xhl, hscr, hs1, hs2, hssq, hsvar, hsrstd, hsdone)
            p.wait(nc.tensor, hsdone)
            p.wait(nc.tensor, xtdone)
            for k in range(KD):
                p.wait(nc.tensor, hxt)
                p.inc(nc.tensor.transpose(p.pst_ln[0][:, :], xhl[:, k * 128:(k + 1) * 128],
                                          ident[:, :]), htr)
                p.wait(nc.vector, htr)
                p.inc(nc.vector.tensor_copy(xthb[:, k * 4:k * 4 + 3], p.pst_ln[0][:, 0:3]),
                      hxt)

        wbuf = [p.sb(f"wbuf{i}", [128, KD * 512], BF16) for i in range(2)]
        evac = p.sb("evac", [128, NT * 512], BF16)
        hevac = p.sb("hevac", [128, TOK], BF16)
        hevac2 = p.sb("hevac2", [128, 4], BF16)
        devac = p.sb("devac", [128, TOK], F32)
        psg = [p.ps(f"psg{i}", [128, 512], F32) for i in range(4)]

        wl = [p.sem(f"wl{i}") for i in range(2)]
        mmd = p.sem("mmd")
        evs = p.sem("evs")
        sto = p.sem("sto")

        p.wait(nc.tensor, xtdone)

        NOF_G = INTER // 512
        for j in range(NOF_G):
            b = j % 2
            p.wait(nc.gpsimd, mmd, slack=0 if j < 2 else NT)
            src = w_in[:, j * 512:(j + 1) * 512].rearrange("(k r) c -> r k c", r=128)
            p.dma(nc.gpsimd, wbuf[b][:, :].rearrange("r (k c) -> r k c", c=512), src, wl[b])
            p.wait(nc.tensor, wl[b])
            for t in range(NT):
                bank = t % 4
                p.wait(nc.tensor, evs, slack=0 if (j == 0 and t < 4) else 3)
                for k in range(KD):
                    mm = nc.tensor.matmul(psg[bank][:, :], xt[k][:, t * 128:(t + 1) * 128],
                                          wbuf[b][:, k * 512:(k + 1) * 512],
                                          start=(k == 0), stop=(k == KD - 1))
                p.inc(mm, mmd)
                p.wait(nc.vector, mmd)
                p.wait(nc.vector, sto)
                p.inc(nc.vector.tensor_copy(evac[:, t * 512:(t + 1) * 512], psg[bank][:, :]),
                      evs)
            p.wait(nc.sync, evs)
            dst = gate_o[:, j * 512:(j + 1) * 512].rearrange("(u r) c -> r u c", r=128)
            p.dma(nc.sync, dst, evac[:, :].rearrange("r (u c) -> r u c", c=512), sto)

        NCH = CONV_DIM // 128
        for j in range(NCH + 1):
            b = j % 2
            isdt = (j == NCH)
            col0 = INTER + j * 128
            p.wait(nc.gpsimd, mmd, slack=0 if j < 2 else 2)
            src = w_in[:, col0:col0 + 128].rearrange("(k r) c -> r k c", r=128)
            p.dma(nc.gpsimd, wbuf[b][:, 0:KD * 128].rearrange("r (k c) -> r k c", c=128),
                  src, wl[b])
            p.wait(nc.tensor, wl[b])
            for half in range(2):
                bank = half % 4
                p.wait(nc.tensor, evs, slack=0 if j == 0 else 1)
                for k in range(KD):
                    mm = nc.tensor.matmul(psg[bank][:, :], wbuf[b][:, k * 128:(k + 1) * 128],
                                          xt[k][:, half * 512:(half + 1) * 512],
                                          start=(k == 0), stop=(k == KD - 1))
                p.inc(mm, mmd)
                p.wait(nc.vector, mmd)
                p.wait(nc.vector, sto)
                dstb = devac if isdt else hevac
                p.inc(nc.vector.tensor_copy(dstb[:, half * 512:(half + 1) * 512],
                                            psg[bank][:, :]), evs)
            if xh_in is not None and not isdt:
                p.wait(nc.tensor, evs, slack=0 if j == 0 else 1)
                p.wait(nc.tensor, hxt)
                for k in range(KD):
                    mm = nc.tensor.matmul(psg[2][:, 0:3], wbuf[b][:, k * 128:(k + 1) * 128],
                                          xthb[:, k * 4:k * 4 + 3],
                                          start=(k == 0), stop=(k == KD - 1))
                p.inc(mm, mmd)
                p.wait(nc.vector, mmd)
                p.inc(nc.vector.tensor_copy(hevac2[:, 0:3], psg[2][:, 0:3]), evs)
            p.wait(nc.sync, evs)
            if isdt:
                p.dma(nc.sync, dt_o[:, :], devac[:, :], sto)
            else:
                if xh_in is not None:
                    p.dma(nc.sync, hbc_o[j * 128:(j + 1) * 128, 0:3], hevac2[:, 0:3], sto)
                p.dma(nc.sync, hbc_o[j * 128:(j + 1) * 128, hof:hof + TOK], hevac[:, :], sto)
        p.wait(nc.sync, sto)


def build_A1():
    p = P("a1")
    nc = p.nc
    x_in = nc.dram_tensor("x", [TOK, D], F32, kind="ExternalInput")
    w_in = nc.dram_tensor("w", [D, PROJ], F32, kind="ExternalInput")
    gate_o = nc.dram_tensor("gate", [TOK, INTER], BF16, kind="ExternalOutput")
    hbc_o = nc.dram_tensor("hbc", [CONV_DIM, TOK], BF16, kind="ExternalOutput")
    dt_o = nc.dram_tensor("dt", [H, TOK], F32, kind="ExternalOutput")
    emit_A1(p, x_in, w_in, gate_o, hbc_o, dt_o)
    return p.nc


def emit_A2(p, hbc_in, dt_in, cw_in, cb_in, dtb_in, al_in, hs_o, ccm_o, acs_o,
            dpre_o, logd_o, e_o, pfx_o, yd_o, phases="cds"):
    nc = p.nc
    with p.stack:
        identb, ident_s = p.make_identity(BF16)
        identf = p.sb("identf", [128, 128], F32)
        nc.gpsimd.memset(identf[:, :], 0.0)
        p.inc(nc.gpsimd.affine_select(identf[:, :], identf[:, :], [[-1, 128]],
                                      ALU.not_equal, 1.0, channel_multiplier=1), ident_s)
        uincl = p.sb("uincl", [128, 128], F32)
        nc.gpsimd.memset(uincl[:, :], 1.0)
        p.inc(nc.gpsimd.affine_select(uincl[:, :], uincl[:, :], [[1, 128]],
                                      ALU.is_ge, 0.0, channel_multiplier=-1), ident_s)
        su = p.sb("su", [128, 128], F32)
        nc.gpsimd.memset(su[:, :], 1.0)
        p.inc(nc.gpsimd.affine_select(su[:, :], su[:, :], [[-1, 128]],
                                      ALU.is_gt, 0.0, channel_multiplier=1), ident_s)
        maskup = p.sb("maskup", [128, 128], F32)
        nc.gpsimd.memset(maskup[:, :], 1.0)
        p.inc(nc.gpsimd.affine_select(maskup[:, :], maskup[:, :], [[1, 128]],
                                      ALU.is_ge, 0.0, channel_multiplier=-1), ident_s)
        onesf = p.sb("onesf", [128, 128], F32)
        p.inc(nc.gpsimd.memset(onesf[:, :], 1.0), ident_s)
        cm1, cp1 = p.ccol(-1.0), p.ccol(1.0)

        pGm = p.ps("pGm", [128, 512], F32)
        pAcb = pGm.bitcast(BF16)
        pA = pGm
        pSt = [p.ps(f"pSt{i}", [128, 512], F32) for i in range(3)]
        pY = [p.ps(f"pY{i}", [128, 512], F32) for i in range(2)]
        pSn = [p.ps(f"pSn{i}", [128, 512], F32) for i in range(2)]
        pAb = pA[:, 0:128]
        pAc = pAcb[:, 0:128]

        btm = [p.sb(f"btm{g}", [128, NCHUNK * 128], BF16) for g in range(G)]
        bcm = [p.sb(f"bcm{g}", [128, TOK], BF16) for g in range(G)]
        ccm = [p.sb(f"ccm{g}", [128, TOK], BF16) for g in range(G)]
        xsb = p.sb("xsb", [128, INTER], BF16)
        prev = p.sb("prev", [128, INTER], F32)
        ystage = p.sb("ystage", [128, 2048], F32)

        hbct = [p.sb(f"hbct{i}", [128, TOK + 3], BF16) for i in range(2)]
        cacc = [p.sb(f"cacc{i}", [128, TOK], F32) for i in range(2)]
        cwt = [p.sb(f"cwt{i}", [128, 4], F32) for i in range(2)]
        cbt = p.sb("cbt", [128, CONV_DIM // 128], F32)
        chbuf = [p.sb(f"chbuf{i}", [128, TOK], BF16) for i in range(2)]
        htst = p.sb("htst", [128, TOK], BF16)

        dts = [p.sb(f"dts{i}", [128, TOK], F32) for i in range(3)]
        dtsp = p.sb("dtsp", [128, TOK], F32)
        adt = p.sb("adt", [128, TOK], F32)
        dtbc = p.sb("dtbc", [128, 1], F32)
        alc = p.sb("alc", [128, 1], F32)
        acol = p.sb("acol", [128, 1], F32)
        adttm = p.sb("adttm", [128, TOK], F32)
        dttm = p.sb("dttm", [128, TOK], F32)
        acssb = p.sb("acssb", [128, TOK], F32)
        tbcsb = p.sb("tbcsb", [128, TOK], F32)
        dssb = p.sb("dssb", [128, TOK], F32)
        expt = p.sb("expt", [128, TOK], F32)
        dpres = p.sb("dpres", [128, TOK], F32)
        rsum = p.sb("rsum", [128, 128], F32)

        gmm = p.sb("gmm", [128, G * 128], F32)
        w2 = [p.sb(f"w2_{i}", [128, 512], F32) for i in range(4)]
        exps = [p.sb(f"expS{i}", [128, 512], F32) for i in range(4)]
        mst = [p.sb(f"mst{i}", [128, 512], BF16) for i in range(4)]
        bdb = [p.sb(f"bdb{i}", [128, 512], BF16) for i in range(4)]

        ld = p.sem("ld"); hld = p.sem("hld"); cvd = p.sem("cvd")
        trd = p.sem("trd"); evd = p.sem("evd"); hsto = p.sem("hsto")
        dtd = p.sem("dtd"); xld = p.sem("xld"); xbd = p.sem("xbd")
        gmd = p.sem("gmd"); gme = p.sem("gme")
        w2d = p.sem("w2d"); std = p.sem("std"); expd = p.sem("expd")
        msd = p.sem("msd"); ymd = p.sem("ymd"); yed = p.sem("yed")
        ysto = p.sem("ysto"); bdd = p.sem("bdd"); snd = p.sem("snd")
        pvd = p.sem("pvd"); pfc = p.sem("pfc"); pfs = p.sem("pfs")
        esto = p.sem("esto")

        p.dma(nc.sync, cbt[:, :], cb_in[:, :], ld)
        p.dma(nc.sync, dtbc[:, :], dtb_in[:, :], ld)
        p.dma(nc.sync, alc[:, :], al_in[:, :], ld)

        NCH = CONV_DIM // 128
        htst2 = [htst, p.sb("htstB", [128, TOK], BF16)]
        silu_after, cons_after, store_after = [], [], []
        for j in range(NCH):
            b = j % 2
            if j >= 2:
                nc.sync.wait_ge(cvd, silu_after[j - 2])
            p.dma(nc.sync, hbct[b][:, :], hbc_in[j * 128:(j + 1) * 128, :], hld)
            p.dma(nc.sync, cwt[b][:, :], cw_in[j * 128:(j + 1) * 128, :], hld)
            p.wait(nc.vector, hld)
            if j >= 2:
                nc.vector.wait_ge(cvd, silu_after[j - 2])
            nc.vector.tensor_scalar(cacc[b][:, :], hbct[b][:, 0:TOK], cwt[b][:, 0:1],
                                    None, ALU.mult)
            for k in range(1, 4):
                ins = nc.vector.scalar_tensor_tensor(cacc[b][:, :], hbct[b][:, k:k + TOK],
                                                     cwt[b][:, k:k + 1], cacc[b][:, :],
                                                     ALU.mult, ALU.add)
            p.inc(ins, cvd)
            p.wait(nc.scalar, cvd)
            p.wait(nc.scalar, ld)
            if j >= 2:
                nc.scalar.wait_ge(evd, cons_after[j - 2])
                nc.scalar.wait_ge(hsto, store_after[j - 2])
            p.inc(nc.scalar.activation(chbuf[b][:, :], cacc[b][:, :], AF.Silu,
                                       bias=cbt[:, j:j + 1]), cvd)
            silu_after.append(p.tot[cvd.name])
            if j < 72:
                p.wait(nc.tensor, cvd)
                p.wait(nc.tensor, ident_s)
                for tb in range(NT):
                    pslot = pAc
                    p.wait(nc.tensor, evd)
                    p.inc(nc.tensor.transpose(pslot, chbuf[b][:, tb * 128:(tb + 1) * 128],
                                              identb[:, :]), trd)
                    p.wait(nc.vector, trd)
                    if j < 64:
                        if j >= 2:
                            nc.vector.wait_ge(hsto, store_after[j - 2])
                        dstt = htst2[b][:, tb * 128:(tb + 1) * 128]
                    else:
                        dstt = btm[j - 64][:, tb * 128:(tb + 1) * 128]
                    p.inc(nc.vector.tensor_copy(dstt, pslot), evd)
                if j < 64:
                    p.wait(nc.sync, evd)
                    dst = hs_o[:, j * 128:(j + 1) * 128].rearrange("(tb q) c -> q tb c", q=128)
                    p.dma(nc.sync, dst, htst2[b][:, :].rearrange("q (tb c) -> q tb c", c=128), hsto)
                else:
                    p.wait(nc.gpsimd, cvd)
                    p.inc(nc.gpsimd.tensor_scalar(bcm[j - 64][:, :], chbuf[b][:, :],
                                                  1.0, None, ALU.mult), evd)
            else:
                gc = j - 72
                p.wait(nc.gpsimd, cvd)
                p.inc(nc.gpsimd.tensor_scalar(ccm[gc][:, :], chbuf[b][:, :],
                                              1.0, None, ALU.mult), evd)
                p.wait(nc.sync, cvd)
                p.dma(nc.sync, ccm_o[gc * 128:(gc + 1) * 128, :], chbuf[b][:, :], hsto)
            cons_after.append(p.tot[evd.name])
            store_after.append(p.tot[hsto.name])

        if "d" not in phases:
            p.wait(nc.sync, hsto)
            p.wait(nc.sync, evd)
            return
        p.dma(nc.sync, dts[2][:, :], dt_in[:, :], ld)
        p.wait(nc.vector, ld)
        nc.vector.tensor_scalar(dts[0][:, :], dts[2][:, :], dtbc[:, 0:1], None, ALU.add)
        p.incd(nc.vector, dtd)
        p.wait(nc.scalar, dtd)
        p.wait_consts(nc.scalar)
        nc.scalar.activation(dts[1][:, :], dts[0][:, :], AF.Abs)
        nc.scalar.activation(dts[1][:, :], dts[1][:, :], AF.Exp, scale=cm1)
        p.inc(nc.scalar.activation(dts[1][:, :], dts[1][:, :], AF.Ln, bias=cp1), dtd)
        nc.vector.tensor_scalar_max(dts[2][:, :], dts[0][:, :], 0.0)
        p.wait(nc.vector, dtd)
        nc.vector.tensor_add(dtsp[:, :], dts[2][:, :], dts[1][:, :])
        p.incd(nc.vector, dtd)
        p.wait(nc.scalar, dtd)
        nc.scalar.activation(acol[:, 0:1], alc[:, 0:1], AF.Exp)
        p.incd(nc.scalar, dtd)
        p.wait(nc.vector, dtd)
        nc.vector.tensor_scalar(acol[:, 0:1], acol[:, 0:1], -1.0, None, ALU.mult)
        nc.vector.drain()
        nc.vector.tensor_scalar(adt[:, :], dtsp[:, :], acol[:, 0:1], None, ALU.mult)
        nc.vector.memset(rsum[:, :], 0.0)
        p.incd(nc.vector, dtd)
        for c in range(NCHUNK):
            cc = c * 128
            p.wait(nc.tensor, dtd)
            p.inc(nc.tensor.transpose(pAb, adt[:, cc:cc + 128], identf[:, :]), trd)
            p.wait(nc.vector, trd)
            p.inc(nc.vector.tensor_copy(adttm[:, cc:cc + 128], pAb), dtd)
            p.wait(nc.tensor, dtd)
            p.inc(nc.tensor.transpose(pAb, dtsp[:, cc:cc + 128], identf[:, :]), trd)
            p.wait(nc.vector, trd)
            p.inc(nc.vector.tensor_copy(dttm[:, cc:cc + 128], pAb), dtd)
            p.wait(nc.tensor, dtd)
            p.inc(nc.tensor.matmul(pGm[:, 0:128], uincl[:, :], adttm[:, cc:cc + 128],
                                   start=True, stop=True), trd)
            p.inc(nc.tensor.matmul(pGm[:, 128:256], onesf[:, :], adttm[:, cc:cc + 128],
                                   start=True, stop=True), trd)
            p.wait(nc.vector, trd)
            nc.vector.tensor_copy(acssb[:, cc:cc + 128], pGm[:, 0:128])
            nc.vector.tensor_copy(tbcsb[:, cc:cc + 128], pGm[:, 128:256])
            nc.vector.tensor_sub(dssb[:, cc:cc + 128], tbcsb[:, cc:cc + 128],
                                 acssb[:, cc:cc + 128])
            p.incd(nc.vector, dtd)
            p.wait(nc.scalar, dtd)
            nc.scalar.activation(dssb[:, cc:cc + 128], dssb[:, cc:cc + 128], AF.Exp)
            nc.scalar.activation(expt[:, cc:cc + 128], tbcsb[:, cc:cc + 128], AF.Exp)
            p.inc(nc.scalar.activation(dpres[:, cc:cc + 128], rsum[:, :], AF.Exp), dtd)
            p.wait(nc.vector, dtd)
            p.inc(nc.vector.tensor_add(rsum[:, :], rsum[:, :], tbcsb[:, cc:cc + 128]), dtd)
        p.incd(nc.vector, dtd)
        p.wait(nc.sync, dtd)
        p.dma(nc.sync, acs_o[:, :].rearrange("(c l) h -> l c h", l=128),
              acssb[:, :].rearrange("l (c h) -> l c h", h=128), esto)
        p.dma(nc.sync, dpre_o[:, :], dpres[:, :], esto)
        p.dma(nc.sync, logd_o[:, :], rsum[0:1, :], esto)

        if "s" not in phases:
            p.wait(nc.sync, hsto)
            p.wait(nc.sync, esto)
            return
        nc.vector.memset(prev[:, :], 0.0)
        p.incd(nc.vector, pvd)
        p.wait(nc.gpsimd, dtd)
        p.wait(nc.gpsimd, evd)
        p.wait(nc.tensor, evd)
        NIT = NCHUNK * 32

        def emit_w2st(it):
            c2_, hb2 = divmod(it, 32)
            h2 = hb2 * 4
            b4 = it % 4
            p.wait(nc.gpsimd, std, slack=0 if it < 4 else 3)
            for i in range(4):
                ins = nc.gpsimd.tensor_scalar(
                    w2[b4][:, i * 128:(i + 1) * 128], uincl[:, :],
                    adttm[:, c2_ * 128 + h2 + i:c2_ * 128 + h2 + i + 1], None, ALU.mult)
            p.inc(ins, w2d)
            p.wait(nc.tensor, w2d)
            if it >= 3:
                nc.tensor.wait_ge(expd, it - 2)
            p.inc(nc.tensor.matmul(pSt[it % 3][:, :], su[:, :], w2[b4][:, :],
                                   start=True, stop=True), std)

        it = 0
        for c in range(NCHUNK):
            cc = c * 128
            p.wait(nc.sync, hsto)
            p.wait(nc.sync, ymd)
            p.wait(nc.sync, snd)
            p.dma(nc.sync, xsb[:, :], hs_o[cc:cc + 128, :], xld)
            p.wait(nc.gpsimd, pvd)
            p.dma(nc.gpsimd, pfx_o[cc:cc + 128, :], prev[:, :], pfs)
            p.wait(nc.vector, xld)
            dtap = dttm[:, cc:cc + 128].unsqueeze(2).broadcast_to((128, 128, Pd))
            p.inc(nc.vector.tensor_tensor(xsb[:, :].rearrange("l (h q) -> l h q", q=Pd),
                                          xsb[:, :].rearrange("l (h q) -> l h q", q=Pd),
                                          dtap, ALU.mult), xbd)
            for g in range(G):
                p.wait(nc.tensor, gme)
                p.wait(nc.tensor, dtd)
                p.inc(nc.tensor.matmul(pGm[:, 0:128], bcm[g][:, cc:cc + 128],
                                       ccm[g][:, cc:cc + 128], start=True, stop=True), gmd)
                p.wait(nc.vector, gmd)
                p.inc(nc.vector.tensor_tensor(gmm[:, g * 128:(g + 1) * 128], pGm[:, 0:128],
                                              maskup[:, :], ALU.mult), gme)
            p.wait(nc.tensor, xbd)
            p.wait(nc.tensor, gme)
            p.wait(nc.vector, pfs)
            for hb in range(32):
                h0 = hb * 4
                g0m = h0 % G
                b2 = it % 2
                b4 = it % 4
                if it == 0:
                    emit_w2st(0)
                    emit_w2st(1)
                if it + 2 < NIT:
                    emit_w2st(it + 2)
                nc.scalar.wait_ge(std, it + 1)
                p.inc(nc.scalar.activation(exps[b4][:, :], pSt[it % 3][:, :], AF.Exp),
                      expd)
                p.wait(nc.vector, expd)
                p.wait(nc.vector, ymd, slack=0 if it < 4 else 12)
                p.inc(nc.vector.tensor_tensor(
                    mst[b4][:, :], exps[b4][:, :],
                    gmm[:, g0m * 128:(g0m + 4) * 128], ALU.mult), msd)
                p.wait(nc.tensor, msd)
                p.wait(nc.tensor, yed, slack=0 if it < 2 else 1)
                for i in range(4):
                    mm = nc.tensor.matmul(pY[b2][:, i * 64:(i + 1) * 64],
                                          mst[b4][:, i * 128:(i + 1) * 128],
                                          xsb[:, (h0 + i) * Pd:(h0 + i + 1) * Pd],
                                          start=True, stop=True)
                    p.inc(mm, ymd)
                p.wait(nc.gpsimd, snd, slack=0 if it < 4 else 12)
                for i in range(4):
                    ins = nc.gpsimd.tensor_scalar(
                        bdb[b4][:, i * 128:(i + 1) * 128],
                        btm[(h0 + i) % G][:, cc:cc + 128],
                        dssb[:, cc + h0 + i:cc + h0 + i + 1], None, ALU.mult)
                p.inc(ins, bdd)
                p.wait(nc.tensor, bdd)
                p.wait(nc.tensor, pvd, slack=0 if it < 2 else 2)
                for i in range(4):
                    mm = nc.tensor.matmul(pSn[b2][:, i * 64:(i + 1) * 64],
                                          bdb[b4][:, i * 128:(i + 1) * 128],
                                          xsb[:, (h0 + i) * Pd:(h0 + i + 1) * Pd],
                                          start=True, stop=True)
                    p.inc(mm, snd)
                p.wait(nc.vector, ymd)
                if hb % 8 == 0:
                    p.wait(nc.vector, ysto)
                p.inc(nc.vector.tensor_copy(ystage[:, (hb % 8) * 256:(hb % 8 + 1) * 256],
                                            pY[b2][:, 0:256]), yed)
                if hb % 8 == 7:
                    p.wait(nc.sync, yed)
                    p.dma(nc.sync, yd_o[cc:cc + 128, (hb - 7) * 256:(hb + 1) * 256],
                          ystage[:, :], ysto)
                p.wait(nc.vector, snd)
                for i in range(4):
                    ins = nc.vector.scalar_tensor_tensor(
                        prev[:, (h0 + i) * Pd:(h0 + i + 1) * Pd],
                        prev[:, (h0 + i) * Pd:(h0 + i + 1) * Pd],
                        expt[:, cc + h0 + i:cc + h0 + i + 1],
                        pSn[b2][:, i * 64:(i + 1) * 64],
                        ALU.mult, ALU.add)
                p.inc(ins, pvd, 2)
                it += 1
                p.wait(nc.sync, pvd)
        p.dma(nc.sync, e_o[:, :], prev[:, :], esto)
        p.wait(nc.sync, esto)
        p.wait(nc.sync, ysto)
        p.wait(nc.sync, pfs)


def build_A2(phases="cds"):
    p = P("a2")
    nc = p.nc
    hbc_in = nc.dram_tensor("hbc", [CONV_DIM, TOK + 3], BF16, kind="ExternalInput")
    dt_in = nc.dram_tensor("dt", [H, TOK], F32, kind="ExternalInput")
    cw_in = nc.dram_tensor("cw", [CONV_DIM, 4], F32, kind="ExternalInput")
    cb_in = nc.dram_tensor("cb", [128, CONV_DIM // 128], F32, kind="ExternalInput")
    dtb_in = nc.dram_tensor("dtb", [128, 1], F32, kind="ExternalInput")
    al_in = nc.dram_tensor("al", [128, 1], F32, kind="ExternalInput")
    hs_o = nc.dram_tensor("hs", [TOK, INTER], BF16, kind="ExternalOutput")
    ccm_o = nc.dram_tensor("ccm", [G * Nd, TOK], BF16, kind="ExternalOutput")
    acs_o = nc.dram_tensor("acs", [TOK, H], F32, kind="ExternalOutput")
    dpre_o = nc.dram_tensor("dpre", [128, NCHUNK * H], F32, kind="ExternalOutput")
    logd_o = nc.dram_tensor("logd", [1, H], F32, kind="ExternalOutput")
    e_o = nc.dram_tensor("eout", [128, INTER], F32, kind="ExternalOutput")
    pfx_o = nc.dram_tensor("pfx", [NCHUNK * 128, INTER], BF16, kind="ExternalOutput")
    yd_o = nc.dram_tensor("ydiag", [TOK, INTER], F32, kind="ExternalOutput")
    emit_A2(p, hbc_in, dt_in, cw_in, cb_in, dtb_in, al_in, hs_o, ccm_o, acs_o,
            dpre_o, logd_o, e_o, pfx_o, yd_o, phases)
    return p.nc


def emit_mm(p, IN, OUT, ln, dequant, lora, scale_in, io, OFW=512):
    nc = p.nc
    KIN = IN // 128
    x_in = io["x"]; res_in = io["res"]; bias_in = io["bias"]; out_o = io["out"]
    if dequant:
        qw_in = io["qw"]; sc_in = io["sc"]; zr_in = io["zr"]; wdq_d = io["wdq"]
    else:
        w_in = io["w"]
    if scale_in:
        nw_in = io["nw"]
    if lora:
        lsi_in = io["lsi"]; bid_in = io["bid"]; lat_in = io["lat"]; lbt_in = io["lbt"]
    with p.stack:
        ident, ident_s = p.make_identity(BF16)
        p.pst_ln = [p.ps("pst_ln0", [128, 128], BF16), p.ps("pst_ln1", [128, 128], BF16)]
        xt = [p.sb(f"xt{k}", [128, TOK], BF16) for k in range(KIN)]
        onesb = p.sb("onesb", [1, 128], BF16)
        p.inc(nc.gpsimd.memset(onesb[:, :], 1.0), ident_s)
        NOF = OUT // OFW
        NWB = 1 if IN >= 8192 else 2
        wbuf = [p.sb(f"wbuf{i}", [128, KIN * OFW], BF16) for i in range(NWB)]
        evac = p.sb("evac", [128, 2 * OFW], F32)
        resb = p.sb("resb", [128, 2 * OFW], F32)
        biasb = p.sb("biasb", [1, OUT], BF16)
        psg = [p.ps(f"psg{i}", [128, 512], F32) for i in range(4)]

        wl = [p.sem(f"wl{i}") for i in range(2)]
        mmd = p.sem("mmd")
        evs = p.sem("evs")
        sto = p.sem("sto")
        rld = p.sem("rld")
        bld = p.sem("bld")
        p.dma(nc.gpsimd, biasb[:, :], bias_in[:, :], bld)

        if ln:
            xtdone = ln_and_transpose(p, x_in, xt, ident, ident_s)
        else:
            if scale_in:
                nwb = p.sb("nwb", [128, KIN], F32)
                p.dma(nc.sync, nwb[:, :], nw_in[:, :], bld)
                p.wait(nc.vector, bld)
                xtdone = load_and_transpose(p, x_in, xt, ident, ident_s, scale_sb=nwb)
            else:
                xtdone = load_and_transpose(p, x_in, xt, ident, ident_s)

        if lora:
            lsb = p.sb("lsb", [1, 4], I32)
            lsf = p.sb("lsf", [1, 16], F32)
            coef = p.sb("coef", [1, 4], F32)
            coefbc = p.sb("coefbc", [128, 4], F32)
            onesf1 = p.sb("onesf1", [1, 128], F32)
            p.inc(nc.gpsimd.memset(onesf1[:, :], 1.0), ident_s)
            labig = [p.sb(f"labig{j}", [128, KIN * 16], F32) for j in range(3)]
            laTall = p.sb("laTall", [128, KIN * 16], BF16)
            lbbig = p.ln_scr[0:16, 0:OUT]
            lbT = p.sb("lbT", [16, OUT], BF16)
            lbTf = p.ln_xrow[0:16, 0:OUT]
            lo1 = p.sb("lo1", [128, 16], BF16)
            lo1t = p.sb("lo1t", [16, TOK], BF16)
            ploA = p.ps("ploA", [128, 16], F32)
            ploT = p.ps("ploT", [16, 128], BF16)
            lod = p.sem("lod")
            lmm = p.sem("lmm")
            p.dma(nc.sync, lsb[:, :], lsi_in[:, :], lod)
            p.dma(nc.sync, lsf[:, 8:9], bid_in[:, :], lod)
            for j3 in range(3):
                p.dma(nc.sync, labig[j3][:, :].rearrange("r (k c) -> r k c", c=16),
                      lat_in[j3].rearrange("(k r) c -> r k c", r=128), lod)
            lbld = p.sem("lbld")
            p.wait(nc.sync, xtdone)
            p.dma(nc.sync, lbbig[:, :], lbt_in[0], lbld)
            p.wait(nc.vector, lod)
            nc.vector.tensor_scalar(lsf[:, 0:4], lsb[:, :], 1.0, None, ALU.mult)
            nc.vector.drain()
            nc.vector.tensor_scalar(lsf[:, 9:10], lsf[:, 8:9], 0.5, None, ALU.add)
            nc.vector.drain()
            nc.vector.tensor_tensor(lsf[:, 4:8], lsf[:, 0:4],
                                    lsf[:, 9:10].broadcast_to((1, 4)), ALU.is_lt)
            nc.vector.drain()
            nc.vector.reduce_sum(lsf[:, 10:11], lsf[:, 4:8], axis=AX)
            nc.vector.drain()
            nc.vector.tensor_scalar(lsf[:, 11:12], lsf[:, 10:11], 1.5, None, ALU.is_gt)
            nc.vector.tensor_scalar(lsf[:, 12:13], lsf[:, 10:11], -2.0, 0.0, ALU.add, ALU.max)
            nc.vector.drain()
            nc.vector.tensor_scalar(lsf[:, 12:13], lsf[:, 12:13], 2.0, None, ALU.min)
            nc.vector.drain()
            for k3 in range(3):
                nc.vector.tensor_scalar(lsf[:, 13:14], lsf[:, 12:13], -float(k3), None, ALU.add)
                nc.vector.drain()
                nc.vector.tensor_tensor(lsf[:, 13:14], lsf[:, 13:14], lsf[:, 13:14], ALU.mult)
                nc.vector.drain()
                nc.vector.tensor_scalar(lsf[:, 13:14], lsf[:, 13:14], 0.25, None, ALU.is_lt)
                nc.vector.drain()
                nc.vector.tensor_tensor(coef[:, k3:k3 + 1], lsf[:, 13:14],
                                        lsf[:, 11:12], ALU.mult)
                nc.vector.drain()
            p.incd(nc.vector, lod)
            p.wait(nc.tensor, lod)
            p.wait(nc.tensor, ident_s)
            p.inc(nc.tensor.matmul(ploA[:, 0:4], onesf1[:, :], coef[:, 0:4],
                                   start=True, stop=True), lod)
            p.wait(nc.vector, lod)
            p.inc(nc.vector.tensor_copy(coefbc[:, :], ploA[:, 0:4]), lod)
            nc.vector.drain()
            nc.vector.tensor_scalar(laTall[:, :], labig[0][:, :], coefbc[:, 0:1],
                                    None, ALU.mult)
            for j3 in (1, 2):
                nc.vector.scalar_tensor_tensor(laTall[:, :], labig[j3][:, :],
                                               coefbc[:, j3:j3 + 1], laTall[:, :],
                                               ALU.mult, ALU.add)
            p.wait(nc.vector, lbld)
            nc.vector.tensor_scalar(lbTf[:, :], lbbig[:, :], coefbc[0:16, 0:1],
                                    None, ALU.mult)
            p.incd(nc.vector, lod)
            for j3 in (1, 2):
                p.wait(nc.sync, lod)
                p.dma(nc.sync, lbbig[:, :], lbt_in[j3], lbld)
                p.wait(nc.vector, lbld)
                nc.vector.scalar_tensor_tensor(lbTf[:, :], lbbig[:, :],
                                               coefbc[0:16, j3:j3 + 1], lbTf[:, :],
                                               ALU.mult, ALU.add)
                p.incd(nc.vector, lod)
            nc.vector.tensor_copy(lbT[:, :], lbTf[:, :])
            p.incd(nc.vector, lod)
            p.wait(nc.tensor, xtdone)
            p.wait(nc.tensor, lod)
            for t in range(NT):
                for k in range(KIN):
                    mm = nc.tensor.matmul(ploA[:, 0:16], xt[k][:, t * 128:(t + 1) * 128],
                                          laTall[:, k * 16:(k + 1) * 16],
                                          start=(k == 0), stop=(k == KIN - 1))
                p.inc(mm, lmm)
                p.wait(nc.vector, lmm)
                p.inc(nc.vector.tensor_copy(lo1[:, :], ploA[:, 0:16]), lmm)
                p.wait(nc.tensor, lmm)
                p.inc(nc.tensor.transpose(ploT[:, :], lo1[:, :], ident[:, :]), lmm)
                p.wait(nc.vector, lmm)
                p.inc(nc.vector.tensor_copy(lo1t[:, t * 128:(t + 1) * 128], ploT[:, :]),
                      lmm)
            p.wait(nc.tensor, lmm)

        if dequant:
            IH = IN // 2
            qb = p.sb("qb", [128, IH], I32)
            scb = p.sb("scb", [128, IN // 128], F32)
            zrb = p.sb("zrb", [128, IN // 128], F32)
            wdq = p.sb("wdq", [128, IH], BF16)
            dqd = p.sem("dqd")
            dqs = p.sem("dqs")
            for j in range(OUT // 128):
                p.dma(nc.sync, scb[:, :], sc_in[j * 128:(j + 1) * 128, :], dqs)
                p.dma(nc.sync, zrb[:, :], zr_in[j * 128:(j + 1) * 128, :], dqs)
                for hf in range(2):
                    g0 = hf * (IH // 128)
                    p.wait(nc.sync, dqd)
                    p.dma(nc.sync, qb[:, :],
                          qw_in[j * 128:(j + 1) * 128, hf * IH:(hf + 1) * IH], dqs)
                    p.wait(nc.vector, dqs)
                    p.wait(nc.vector, dqd)
                    for g in range(IH // 128):
                        nc.vector.tensor_scalar(wdq[:, g * 128:(g + 1) * 128],
                                                qb[:, g * 128:(g + 1) * 128],
                                                scb[:, g0 + g:g0 + g + 1],
                                                zrb[:, g0 + g:g0 + g + 1],
                                                ALU.mult, ALU.add)
                    p.incd(nc.vector, dqd)
                    p.wait(nc.sync, dqd)
                    p.dma(nc.sync, wdq_d[j * 128:(j + 1) * 128, hf * IH:(hf + 1) * IH],
                          wdq[:, :], dqd)

        p.wait(nc.tensor, xtdone)
        for j in range(NOF):
            b = j % NWB
            if dequant:
                p.wait(nc.sync, mmd, slack=0 if j < NWB else (NWB - 1) * NT)
                p.wait(nc.sync, dqd)
                for k in range(KIN):
                    tsrc = wdq_d[j * OFW:(j + 1) * OFW, k * 128:(k + 1) * 128]
                    p.inc(nc.sync.dma_start_transpose(
                        wbuf[b][:, k * OFW:(k + 1) * OFW], tsrc), wl[b], 16)
            else:
                p.wait(nc.gpsimd, mmd, slack=0 if j < NWB else (NWB - 1) * NT)
                wsrc = w_in[:, j * OFW:(j + 1) * OFW].rearrange("(k r) c -> r k c", r=128)
                p.dma(nc.gpsimd, wbuf[b][:, :].rearrange("r (k c) -> r k c", c=OFW),
                      wsrc, wl[b])
            p.wait(nc.tensor, wl[b])
            p.wait(nc.tensor, bld)
            for t in range(NT):
                bank = t % 4
                u = t % 2
                if u == 0:
                    p.wait(nc.sync, evs)
                    p.dma(nc.sync, resb[:, :].rearrange("r (v c) -> r v c", c=OFW),
                          res_in[t * 128:(t + 2) * 128, j * OFW:(j + 1) * OFW]
                          .rearrange("(v r) c -> r v c", r=128), rld)
                p.wait(nc.tensor, evs, slack=0 if (j == 0 and t < 4) else 3)
                for k in range(KIN):
                    nc.tensor.matmul(psg[bank][:, 0:OFW], xt[k][:, t * 128:(t + 1) * 128],
                                     wbuf[b][:, k * OFW:(k + 1) * OFW],
                                     start=(k == 0), stop=False)
                mm = nc.tensor.matmul(psg[bank][:, 0:OFW], onesb[:, :],
                                      biasb[:, j * OFW:(j + 1) * OFW],
                                      start=False, stop=not lora)
                if lora:
                    mm = nc.tensor.matmul(psg[bank][:, 0:OFW], lo1t[:, t * 128:(t + 1) * 128],
                                          lbT[:, j * OFW:(j + 1) * OFW],
                                          start=False, stop=True)
                p.inc(mm, mmd)
                p.wait(nc.vector, mmd)
                p.wait(nc.vector, rld)
                p.wait(nc.vector, sto)
                p.inc(nc.vector.tensor_tensor(evac[:, u * OFW:(u + 1) * OFW],
                                              psg[bank][:, 0:OFW],
                                              resb[:, u * OFW:(u + 1) * OFW],
                                              ALU.add), evs)
                if u == 1:
                    p.wait(nc.sync, evs)
                    p.dma(nc.sync,
                          out_o[(t - 1) * 128:(t + 1) * 128, j * OFW:(j + 1) * OFW]
                          .rearrange("(v r) c -> r v c", r=128),
                          evac[:, :].rearrange("r (v c) -> r v c", c=OFW), sto)
        p.wait(nc.sync, sto)


def mm_io(nc, pre, IN, OUT, ln, dequant, lora, scale_in):
    io = {"x": nc.dram_tensor(pre + "x", [TOK, IN], F32 if ln else BF16, kind="ExternalInput"),
          "res": nc.dram_tensor(pre + "res", [TOK, OUT], F32, kind="ExternalInput"),
          "bias": nc.dram_tensor(pre + "bias", [1, OUT], F32, kind="ExternalInput"),
          "out": nc.dram_tensor(pre + "out", [TOK, OUT], F32, kind="ExternalOutput")}
    if dequant:
        io["qw"] = nc.dram_tensor(pre + "qw", [OUT, IN], I32, kind="ExternalInput")
        io["sc"] = nc.dram_tensor(pre + "sc", [OUT, IN // 128], F32, kind="ExternalInput")
        io["zr"] = nc.dram_tensor(pre + "zr", [OUT, IN // 128], F32, kind="ExternalInput")
        io["wdq"] = nc.dram_tensor(pre + "wdq_scratch", [OUT, IN], BF16, kind="Internal")
    else:
        io["w"] = nc.dram_tensor(pre + "w", [IN, OUT], F32, kind="ExternalInput")
    if scale_in:
        io["nw"] = nc.dram_tensor(pre + "nw", [128, IN // 128], F32, kind="ExternalInput")
    if lora:
        io["lsi"] = nc.dram_tensor(pre + "lsi", [1, 4], I32, kind="ExternalInput")
        io["bid"] = nc.dram_tensor(pre + "bid", [1, 1], F32, kind="ExternalInput")
        io["lat"] = nc.dram_tensor(pre + "lat", [3, IN, 16], F32, kind="ExternalInput")
        io["lbt"] = nc.dram_tensor(pre + "lbt", [3, 16, OUT], F32, kind="ExternalInput")
    return io


def build_mm(name, IN, OUT, ln, dequant, lora, scale_in, OFW=512):
    p = P(name)
    io = mm_io(p.nc, "", IN, OUT, ln, dequant, lora, scale_in)
    emit_mm(p, IN, OUT, ln, dequant, lora, scale_in, io, OFW)
    return p.nc


def _c(a):
    return np.ascontiguousarray(a)


def kernel(**inputs):
    return kernel_fused(**inputs)


def kernel_unfused(**inputs):
    f32 = np.float32
    hid = np.asarray(inputs["L0_attn_hidden_states"], f32).reshape(8192, D)
    wT = _c(np.asarray(inputs["L0_attn_in_proj_w"], f32).T)
    conv_w = np.asarray(inputs["L0_attn_conv_w"], f32).reshape(CONV_DIM, 4)
    conv_bT = _c(np.asarray(inputs["L0_attn_conv_b"], f32).reshape(80, 128).T)
    dt_bias = _c(np.asarray(inputs["L0_attn_dt_bias"], f32).reshape(128, 1))
    A_log = _c(np.asarray(inputs["L0_attn_A_log"], f32).reshape(128, 1))
    Dv = _c(np.asarray(inputs["L0_attn_D"], f32).reshape(1, H))
    nwT = _c(np.asarray(inputs["L0_attn_norm_weight"], f32).reshape(INTER // 128, 128).T)
    opT = _c(np.asarray(inputs["L0_attn_out_proj_w"], f32).T)
    qw = _c(np.asarray(inputs["L0_ffn_qweight"]).reshape(D, D).astype(np.int32))
    sc = _c(np.asarray(inputs["L0_ffn_scales"], f32).reshape(D, 32))
    zr = _c(np.asarray(inputs["L0_ffn_scaled_zeros"], f32).reshape(D, 32))
    fb = _c(np.asarray(inputs["L0_ffn_bias"], f32).reshape(1, D))
    wqk = np.asarray(inputs["L1_attn_Wqk_w"], f32)
    wqT = _c(wqk[0::2].T)
    wkT = _c(wqk[1::2].T)
    bqk = np.asarray(inputs["L1_attn_Wqk_b"], f32)
    qbT = _c(bqk[0::2].reshape(4, 128).T)
    kbT = _c(bqk[1::2].reshape(4, 128).T)
    wvT = _c(np.asarray(inputs["L1_attn_Wv_w"], f32).T)
    vb = _c(np.asarray(inputs["L1_attn_Wv_b"], f32).reshape(1, 512))
    woT = _c(np.asarray(inputs["L1_attn_Wo_w"], f32).T)
    wob = _c(np.asarray(inputs["L1_attn_Wo_b"], f32).reshape(1, D))
    w1T = _c(np.asarray(inputs["L1_ffn_weight"], f32).T)
    db = _c(np.asarray(inputs["L1_ffn_bias"], f32).reshape(1, D))
    latT = _c(np.asarray(inputs["L1_ffn_lora_A"], f32).transpose(0, 2, 1))
    lbtT = _c(np.asarray(inputs["L1_ffn_lora_B"], f32).transpose(0, 2, 1))
    lsi = _c(np.asarray(inputs["L1_ffn_lora_start_indices"]).reshape(1, 4).astype(np.int32))
    zeros_bias = np.zeros((1, D), f32)
    cores = list(range(NCORES))

    def run(nc, maps):
        return run_bass_kernel_spmd(nc, maps, core_ids=cores).results

    sl = lambda a, c: _c(a[c * TOK:(c + 1) * TOK])

    r1 = run(build_A1(), [{"x": sl(hid, c), "w": wT} for c in cores])
    gate = [np.asarray(r["gate"]) for r in r1]
    hbc = [np.asarray(r["hbc"]) for r in r1]
    dtc = [np.asarray(r["dt"]) for r in r1]

    import ml_dtypes
    BFnp = ml_dtypes.bfloat16
    maps = []
    for c in cores:
        halo = np.zeros((CONV_DIM, 3), BFnp) if c % 4 == 0 else hbc[c - 1][:, -3:]
        maps.append(dict(hbc=_c(np.concatenate([halo, hbc[c]], axis=1)), dt=_c(dtc[c]),
                         cw=conv_w, cb=conv_bT, dtb=dt_bias, al=A_log))
    r2 = run(build_A2(), maps)

    eall = _c(np.concatenate([np.asarray(r["eout"]) for r in r2], axis=0))
    logdall = _c(np.concatenate([np.asarray(r["logd"]) for r in r2], axis=0))
    e8sel = np.zeros((NCORES, NCORES * 128), f32)
    for q in range(NCORES):
        e8sel[q, q * 128:(q + 1) * 128] = 1.0
    maps = []
    for c in cores:
        g0 = 4 * (c // 4)
        grp = range(g0, g0 + 4)
        m8 = np.zeros((NCORES, NCORES), f32)
        mv = np.zeros((NCORES, 1), f32)
        for q in grp:
            if q < c:
                mv[q, 0] = 1.0
                for r in grp:
                    if q < r < c:
                        m8[r, q] = 1.0
        maps.append(dict(ydiag=np.asarray(r2[c]["ydiag"]), hs=np.asarray(r2[c]["hs"]),
                         gate=gate[c], ccm=np.asarray(r2[c]["ccm"]),
                         pfx=np.asarray(r2[c]["pfx"]), acs=np.asarray(r2[c]["acs"]),
                         dpre=np.asarray(r2[c]["dpre"]), eall=eall, logdall=logdall,
                         mask8t=m8, maskv=mv, e8sel=e8sel, dvec=Dv))
    r3a = run(build_A3a(), maps)

    maps = [dict(x=np.asarray(r3a[c]["xn"]), res=sl(hid, c), bias=zeros_bias,
                 w=opT, nw=nwT) for c in cores]
    r3b = run(build_mm("a3b", INTER, D, False, False, False, True, OFW=256), maps)
    x1 = [np.asarray(r["out"]) for r in r3b]

    maps = [dict(x=x1[c], res=x1[c], bias=fb, qw=qw, sc=sc, zr=zr) for c in cores]
    rb = run(build_mm("bffn", D, D, True, True, False, False), maps)
    x2 = [np.asarray(r["out"]) for r in rb]

    maps = [dict(x=x2[c], wq=wqT, wk=wkT, wv=wvT, qb=qbT, kb=kbT, vb=vb) for c in cores]
    rc1 = run(build_C1(), maps)
    kts = [np.asarray(r["kt"]) for r in rc1]
    vs = [np.asarray(r["v"]) for r in rc1]

    maps = []
    for c in cores:
        g0 = 4 * (c // 4)
        ktf = _c(np.concatenate([kts[g0 + i] for i in range(4)], axis=1))
        vf = _c(np.concatenate([vs[g0 + i] for i in range(4)], axis=0))
        maps.append(dict(qt=np.asarray(rc1[c]["qt"]), ktf=ktf, vf=vf, wo=woT,
                         wob=wob, res=x2[c]))
    rc2 = run(build_C2(), maps)
    x3 = [np.asarray(r["out"]) for r in rc2]

    maps = [dict(x=x3[c], res=x3[c], bias=db, w=w1T, lsi=lsi,
                 bid=np.full((1, 1), float(c // 4), f32), lat=latT, lbt=lbtT)
            for c in cores]
    rd = run(build_mm("dffn", D, D, True, False, True, False), maps)
    out = np.concatenate([np.asarray(r["out"]) for r in rd], axis=0)
    return out.reshape(2, 4096, D)


def emit_A3a(p, yd_in, hs_in, gt_in, ccm_in, pfx_in, acs_in, dpre_in, eall_in,
             logd_in, m8_in, mv_in, sel_in, d_in, xn_o, NW=NCORES, dbg=False):
    nc = p.nc
    if dbg:
        dbg_pin = nc.dram_tensor("dbg_pin", [128, INTER], F32, kind="ExternalOutput")
        dbg_y = nc.dram_tensor("dbg_y", [TOK, INTER], F32, kind="ExternalOutput")
        dbg_pvf = nc.dram_tensor("dbg_pvf", [TOK, INTER], F32, kind="ExternalOutput")
        dbg_xg = nc.dram_tensor("dbg_xg", [TOK, INTER], F32, kind="ExternalOutput")
        dbg_gs = nc.dram_tensor("dbg_gs", [TOK, INTER], F32, kind="ExternalOutput")
    with p.stack:
        onesf1 = p.sb("onesf1", [1, 128], F32)
        cst = p.sem("cst0")
        p.inc(nc.gpsimd.memset(onesf1[:, :], 1.0), cst)
        b_eps, s_half = p.ccol(EPS), p.ccol(-0.5)

        pMisc = p.ps("pMisc", [128, 512], F32)
        pY = [p.ps(f"pY{i}", [128, 512], F32) for i in range(2)]

        pin = p.sb("pin", [128, INTER], F32)
        ebuf = p.sb("ebuf", [128, INTER], F32)
        hsb = p.sb("hsb", [128, INTER], BF16)
        gtb = p.sb("gtb", [128, INTER], BF16)
        pfb = p.sb("pfb", [128, INTER], BF16)
        pvf = p.sb("pvf", [128, INTER], BF16)
        sqs = p.sb("sqs", [128, INTER], F32)
        ccmb = [p.sb(f"ccmb{g}", [128, TOK], BF16) for g in range(G)]
        dpreb = p.sb("dpreb", [128, NCHUNK * H], F32)
        coefs = p.sb("coefs", [NW, H], F32)
        coefbc = p.sb("coefbc", [128, NW * 128], F32)
        logdb = p.sb("logdb", [NW, H], F32)
        m8b = p.sb("m8b", [NW, NW], F32)
        mvb = p.sb("mvb", [NW, 1], F32)
        selb = p.sb("selb", [NW, NW * 128], F32)
        dvb = p.sb("dvb", [1, H], F32)
        dbc = p.sb("dbc", [128, H], F32)
        acsb = p.sb("acsb", [128, H], F32)
        sdob = p.sb("sdob", [128, H], F32)
        msb = p.sb("msb", [128, 2 * G], F32)

        ld = p.sem("ld"); eld = p.sem("eld"); cfd = p.sem("cfd")
        pind = p.sem("pind"); chld = p.sem("chld"); pvb = p.sem("pvb")
        ymd = p.sem("ymd"); yev = p.sem("yev"); hbd = p.sem("hbd")
        pld = p.sem("pld"); gnd = p.sem("gnd"); sto = p.sem("sto")

        p.dma(nc.sync, logdb[:, :], logd_in[:, :], ld)
        p.dma(nc.sync, m8b[:, :], m8_in[:, :], ld)
        p.dma(nc.sync, mvb[:, :], mv_in[:, :], ld)
        p.dma(nc.sync, selb[:, :], sel_in[:, :], ld)
        p.dma(nc.sync, dvb[:, :], d_in[:, :], ld)
        p.dma(nc.sync, dpreb[:, :], dpre_in[:, :], ld)
        for g in range(G):
            p.dma(nc.gpsimd, ccmb[g][:, :], ccm_in[g * 128:(g + 1) * 128, :], chld)

        p.wait(nc.tensor, ld)
        p.inc(nc.tensor.matmul(pMisc[0:NW, 0:H], m8b[:, :], logdb[:, :],
                               start=True, stop=True), cfd)
        p.wait(nc.scalar, cfd)
        nc.scalar.activation(coefs[:, :], pMisc[0:NW, 0:H], AF.Exp)
        p.incd(nc.scalar, cfd)
        p.wait(nc.vector, cfd)
        nc.vector.tensor_scalar(coefs[:, :], coefs[:, :], mvb[:, 0:1], None, ALU.mult)
        p.incd(nc.vector, cfd)
        p.wait(nc.tensor, cfd)
        for q in range(NW):
            p.inc(nc.tensor.matmul(pMisc[:, 128:256], selb[:, q * 128:(q + 1) * 128],
                                   coefs[:, :], start=True, stop=True), cfd)
            p.wait(nc.vector, cfd)
            p.inc(nc.vector.tensor_copy(coefbc[:, q * 128:(q + 1) * 128],
                                        pMisc[:, 128:256]), cfd)
            p.wait(nc.tensor, cfd)
        p.wait(nc.tensor, cst)
        p.inc(nc.tensor.matmul(pMisc[:, 256:384], onesf1[:, :], dvb[:, :],
                               start=True, stop=True), cfd)
        p.wait(nc.vector, cfd)
        p.inc(nc.vector.tensor_copy(dbc[:, :], pMisc[:, 256:384]), cfd)

        nc.vector.memset(pin[:, :], 0.0)
        for q in range(NW):
            p.wait(nc.sync, pind)
            p.dma(nc.sync, ebuf[:, :], eall_in[q * 128:(q + 1) * 128, :], eld)
            p.wait(nc.vector, eld)
            cap = coefbc[:, q * 128:(q + 1) * 128].unsqueeze(2).broadcast_to((128, H, Pd))
            nc.vector.tensor_tensor(sqs[:, :].rearrange("n (h q2) -> n h q2", q2=Pd),
                                    ebuf[:, :].rearrange("n (h q2) -> n h q2", q2=Pd),
                                    cap, ALU.mult)
            p.inc(nc.vector.tensor_add(pin[:, :], pin[:, :], sqs[:, :]), pind)

        for c in range(NCHUNK):
            cc = c * 128
            p.wait(nc.sync, pind)
            p.wait(nc.sync, gnd)
            p.wait(nc.gpsimd, gnd)
            p.dma(nc.sync, ebuf[:, :], yd_in[cc:cc + 128, :], chld)
            p.dma(nc.gpsimd, hsb[:, :], hs_in[cc:cc + 128, :], chld)
            p.dma(nc.gpsimd, gtb[:, :], gt_in[cc:cc + 128, :], chld)
            p.dma(nc.gpsimd, pfb[:, :], pfx_in[cc:cc + 128, :], chld)
            p.dma(nc.sync, acsb[:, :], acs_in[cc:cc + 128, :], chld)
            p.wait(nc.scalar, chld)
            p.inc(nc.scalar.activation(sdob[:, :], acsb[:, :], AF.Exp), chld)
            p.wait(nc.vector, chld)
            p.wait(nc.vector, ymd)
            p.wait(nc.vector, sto)
            dap = dpreb[:, c * H:(c + 1) * H].unsqueeze(2).broadcast_to((128, H, Pd))
            nc.vector.tensor_tensor(sqs[:, :].rearrange("n (h q2) -> n h q2", q2=Pd),
                                    pin[:, :].rearrange("n (h q2) -> n h q2", q2=Pd),
                                    dap, ALU.mult)
            p.inc(nc.vector.tensor_add(pvf[:, :], sqs[:, :], pfb[:, :]), pvb)
            if dbg:
                p.wait(nc.sync, pvb)
                p.dma(nc.gpsimd, dbg_pvf[cc:cc + 128, :], pvf[:, :], sto)
                if c == 0:
                    p.wait(nc.sync, pind)
                    p.dma(nc.sync, dbg_pin[:, :], pin[:, :], sto)
            p.wait(nc.tensor, pvb)
            p.wait(nc.tensor, chld)
            for hb in range(32):
                h0 = hb * 4
                b2 = hb % 2
                p.wait(nc.tensor, yev, slack=0 if (c == 0 and hb < 2) else 1)
                for i in range(4):
                    mm = nc.tensor.matmul(pY[b2][:, i * 64:(i + 1) * 64],
                                          ccmb[(h0 + i) % G][:, cc:cc + 128],
                                          pvf[:, (h0 + i) * Pd:(h0 + i + 1) * Pd],
                                          start=True, stop=True)
                    p.inc(mm, ymd)
                sl = slice(h0 * Pd, (h0 + 4) * Pd)
                p.wait(nc.vector, ymd)
                sap = sdob[:, h0:h0 + 4].unsqueeze(2).broadcast_to((128, 4, Pd))
                nc.vector.tensor_tensor(
                    sqs[:, sl].rearrange("l (i q2) -> l i q2", q2=Pd),
                    pY[b2][:, 0:256].rearrange("l (i q2) -> l i q2", q2=Pd),
                    sap, ALU.mult)
                p.inc(nc.vector.tensor_add(ebuf[:, sl], ebuf[:, sl], sqs[:, sl]), yev)
                dapp = dbc[:, h0:h0 + 4].unsqueeze(2).broadcast_to((128, 4, Pd))
                p.wait(nc.gpsimd, yev)
                nc.gpsimd.tensor_tensor(
                    sqs[:, sl].rearrange("l (i q2) -> l i q2", q2=Pd),
                    hsb[:, sl].rearrange("l (i q2) -> l i q2", q2=Pd),
                    dapp, ALU.mult)
                p.inc(nc.gpsimd.tensor_add(ebuf[:, sl], ebuf[:, sl], sqs[:, sl]), hbd)
            if dbg:
                p.wait(nc.sync, hbd)
                p.dma(nc.sync, dbg_y[cc:cc + 128, :], ebuf[:, :], sto)
            p.wait(nc.scalar, hbd)
            if dbg:
                p.wait(nc.scalar, sto)
            p.inc(nc.scalar.activation(gtb[:, :], gtb[:, :], AF.Silu), hbd)
            p.wait(nc.vector, hbd)
            nc.vector.tensor_mul(ebuf[:, :], ebuf[:, :], gtb[:, :])
            if dbg:
                p.incd(nc.vector, hbd)
                p.wait(nc.sync, hbd)
                p.dma(nc.sync, dbg_xg[cc:cc + 128, :], ebuf[:, :], sto)
                p.dma(nc.gpsimd, dbg_gs[cc:cc + 128, :], gtb[:, :], sto)
                p.wait(nc.vector, sto)
            nc.vector.tensor_tensor(sqs[:, :], ebuf[:, :], ebuf[:, :], ALU.mult)
            for g in range(G):
                nc.vector.reduce_sum(msb[:, g:g + 1], sqs[:, g * 1024:(g + 1) * 1024],
                                     axis=AX)
            p.incd(nc.vector, gnd)
            nc.vector.drain()
            nc.vector.tensor_scalar(msb[:, G:2 * G], msb[:, 0:G], 1.0 / 1024.0,
                                    None, ALU.mult)
            p.incd(nc.vector, gnd)
            p.wait(nc.scalar, gnd)
            p.wait_consts(nc.scalar)
            nc.scalar.activation(msb[:, 0:G], msb[:, G:2 * G], AF.Ln, bias=b_eps)
            nc.scalar.drain()
            nc.scalar.activation(msb[:, G:2 * G], msb[:, 0:G], AF.Exp, scale=s_half)
            p.incd(nc.scalar, gnd)
            p.wait(nc.vector, gnd)
            for g in range(G):
                nc.vector.tensor_scalar(pvf[:, g * 1024:(g + 1) * 1024],
                                        ebuf[:, g * 1024:(g + 1) * 1024],
                                        msb[:, G + g:G + g + 1], None, ALU.mult)
            p.incd(nc.vector, gnd)
            p.wait(nc.sync, gnd)
            p.dma(nc.sync, xn_o[cc:cc + 128, :], pvf[:, :], sto)
        p.wait(nc.sync, sto)


def a3a_io(nc, pre, NW=NCORES):
    return [nc.dram_tensor(pre + "ydiag", [TOK, INTER], F32, kind="ExternalInput"),
            nc.dram_tensor(pre + "hs", [TOK, INTER], BF16, kind="ExternalInput"),
            nc.dram_tensor(pre + "gate", [TOK, INTER], BF16, kind="ExternalInput"),
            nc.dram_tensor(pre + "ccm", [G * Nd, TOK], BF16, kind="ExternalInput"),
            nc.dram_tensor(pre + "pfx", [TOK, INTER], BF16, kind="ExternalInput"),
            nc.dram_tensor(pre + "acs", [TOK, H], F32, kind="ExternalInput"),
            nc.dram_tensor(pre + "dpre", [128, NCHUNK * H], F32, kind="ExternalInput"),
            nc.dram_tensor(pre + "eall", [NW * 128, INTER], F32, kind="ExternalInput"),
            nc.dram_tensor(pre + "logdall", [NW, H], F32, kind="ExternalInput"),
            nc.dram_tensor(pre + "mask8t", [NW, NW], F32, kind="ExternalInput"),
            nc.dram_tensor(pre + "maskv", [NW, 1], F32, kind="ExternalInput"),
            nc.dram_tensor(pre + "e8sel", [NW, NW * 128], F32, kind="ExternalInput"),
            nc.dram_tensor(pre + "dvec", [1, H], F32, kind="ExternalInput"),
            nc.dram_tensor(pre + "xn", [TOK, INTER], BF16, kind="ExternalOutput")]


def build_A3a(dbg=False):
    p = P("a3a")
    io = a3a_io(p.nc, "")
    emit_A3a(p, *io, NW=NCORES, dbg=dbg)
    return p.nc


def emit_C1(p, x_in, wq_in, wk_in, wv_in, qb_in, kb_in, vb_in, qt_o, kt_o, v_o):
    nc = p.nc
    HEAD = 512
    with p.stack:
        ident, ident_s = p.make_identity(BF16)
        p.pst_ln = [p.ps("pst_ln0", [128, 128], BF16), p.ps("pst_ln1", [128, 128], BF16)]
        xt = [p.sb(f"xt{k}", [128, TOK], BF16) for k in range(KD)]
        onesb = p.sb("onesb", [1, 128], BF16)
        p.inc(nc.gpsimd.memset(onesb[:, :], 1.0), ident_s)
        wbuf = p.sb("wbuf", [128, KD * HEAD // 8], BF16)
        wbig = p.sb("wbig", [128, KD * 512], BF16)
        qbb = p.sb("qbb", [128, HEAD // 128], F32)
        kbb = p.sb("kbb", [128, HEAD // 128], F32)
        vbb = p.sb("vbb", [1, HEAD], BF16)
        ev = p.sb("ev", [128, 512], BF16)
        psg = [p.ps(f"psg{i}", [128, 512], F32) for i in range(2)]
        wl = p.sem("wl"); bl = p.sem("bl"); mmd = p.sem("mmd")
        evs = p.sem("evs"); sto = p.sem("sto")
        p.dma(nc.sync, qbb[:, :], qb_in[:, :], bl)
        p.dma(nc.sync, kbb[:, :], kb_in[:, :], bl)
        p.dma(nc.gpsimd, vbb[:, :], vb_in[:, :], bl)
        xtdone = ln_and_transpose(p, x_in, xt, ident, ident_s)
        p.wait(nc.tensor, xtdone)
        p.wait(nc.tensor, bl)
        for wi, (w_in_t, bcol, out_t) in enumerate([(wq_in, qbb, qt_o), (wk_in, kbb, kt_o)]):
            p.wait(nc.gpsimd, mmd)
            src = w_in_t[:, :].rearrange("(k r) c -> r k c", r=128)
            p.dma(nc.gpsimd, wbig[:, :].rearrange("r (k c) -> r k c", c=512), src, wl)
            p.wait(nc.tensor, wl)
            for f in range(HEAD // 128):
                for half in range(2):
                    bank = half % 2
                    p.wait(nc.tensor, evs, slack=0 if (wi == 0 and f == 0) else 1)
                    for k in range(KD):
                        mm = nc.tensor.matmul(psg[bank][:, :],
                                              wbig[:, k * 512 + f * 128:k * 512 + (f + 1) * 128],
                                              xt[k][:, half * 512:(half + 1) * 512],
                                              start=(k == 0), stop=(k == KD - 1))
                    p.inc(mm, mmd)
                    p.wait(nc.vector, mmd)
                    p.wait(nc.vector, sto)
                    p.inc(nc.vector.tensor_scalar(ev[:, :], psg[bank][:, :],
                                                  bcol[:, f:f + 1], None, ALU.add), evs)
                    p.wait(nc.sync, evs)
                    p.dma(nc.sync, out_t[f * 128:(f + 1) * 128,
                                         half * 512:(half + 1) * 512], ev[:, :], sto)
        p.wait(nc.gpsimd, mmd)
        src = wv_in[:, :].rearrange("(k r) c -> r k c", r=128)
        p.dma(nc.gpsimd, wbig[:, :].rearrange("r (k c) -> r k c", c=512), src, wl)
        p.wait(nc.tensor, wl)
        for t in range(NT):
            bank = t % 2
            p.wait(nc.tensor, evs, slack=0 if t < 2 else 1)
            for k in range(KD):
                nc.tensor.matmul(psg[bank][:, :], xt[k][:, t * 128:(t + 1) * 128],
                                 wbig[:, k * 512:(k + 1) * 512],
                                 start=(k == 0), stop=False)
            mm = nc.tensor.matmul(psg[bank][:, :], onesb[:, :], vbb[:, :],
                                  start=False, stop=True)
            p.inc(mm, mmd)
            p.wait(nc.vector, mmd)
            p.wait(nc.vector, sto)
            p.inc(nc.vector.tensor_copy(ev[:, :], psg[bank][:, :]), evs)
            p.wait(nc.sync, evs)
            p.dma(nc.sync, v_o[t * 128:(t + 1) * 128, :], ev[:, :], sto)
        p.wait(nc.sync, sto)


def c1_io(nc, pre):
    HEAD = 512
    return [nc.dram_tensor(pre + "x", [TOK, D], F32, kind="ExternalInput"),
            nc.dram_tensor(pre + "wq", [D, HEAD], F32, kind="ExternalInput"),
            nc.dram_tensor(pre + "wk", [D, HEAD], F32, kind="ExternalInput"),
            nc.dram_tensor(pre + "wv", [D, HEAD], F32, kind="ExternalInput"),
            nc.dram_tensor(pre + "qb", [128, HEAD // 128], F32, kind="ExternalInput"),
            nc.dram_tensor(pre + "kb", [128, HEAD // 128], F32, kind="ExternalInput"),
            nc.dram_tensor(pre + "vb", [1, HEAD], F32, kind="ExternalInput"),
            nc.dram_tensor(pre + "qt", [HEAD, TOK], BF16, kind="ExternalOutput"),
            nc.dram_tensor(pre + "kt", [HEAD, TOK], BF16, kind="ExternalOutput"),
            nc.dram_tensor(pre + "v", [TOK, HEAD], BF16, kind="ExternalOutput")]


def build_C1():
    p = P("c1")
    emit_C1(p, *c1_io(p.nc, ""))
    return p.nc


def emit_C2(p, qt_in, kt_in, v_in, wo_in, wob_in, res_in, out_o,
            kt_ranked=False, nh=8, ntt=NT, do_proj=True):
    nc = p.nc
    HEAD = 512
    E = 64
    SEQ = 4096
    ISCL = 1.0 / float(np.sqrt(512.0))
    ksc, ktc, it2 = 0, 0, 0
    with p.stack:
        ident, ident_s = p.make_identity(BF16)
        onesb = p.sb("onesb", [1, 128], BF16)
        p.inc(nc.gpsimd.memset(onesb[:, :], 1.0), ident_s)
        qall = p.sb("qall", [64, 8 * TOK], BF16)
        kh = [p.sb(f"kh{i}", [64, SEQ], BF16) for i in range(2)]
        vh = [p.sb(f"vh{i}", [128, (SEQ // 128) * E], BF16) for i in range(2)]
        ssb = p.sb("ssb", [128, SEQ], F32)
        asb = p.sb("asb", [128, SEQ], BF16)
        aTb = p.sb("aTb", [128, SEQ], BF16)
        xo = p.sb("xo", [128, NT * HEAD], BF16)
        wosb = p.sb("wosb", [128, 4 * D], BF16)
        wobb = p.sb("wobb", [1, D], BF16)
        xoT = p.sb("xoT", [128, 4 * 128], BF16)
        resb = p.sb("resb", [128, D], F32)
        evac = p.sb("evac", [128, D], F32)
        smx = p.sb("smx", [128, 4], F32)
        pS = [p.ps(f"pS{i}", [128, 512], F32) for i in range(2)]
        pTb = [p.ps("pT0", [128, 128], BF16), p.ps("pT1", [128, 128], BF16)]
        pT = pTb[0]
        pO = [p.ps(f"pO{i}", [128, 64], F32) for i in range(2)]
        pP = [p.ps(f"pP{i}", [128, 512], F32) for i in range(2)]

        qld = p.sem("qld"); khl = p.sem("khl"); vhl = p.sem("vhl")
        smm = p.sem("smm"); sev = p.sem("sev"); sfx = p.sem("sfx")
        trd = p.sem("trd"); tev = p.sem("tev"); omm = p.sem("omm")
        oev = p.sem("oev"); wld = p.sem("wld"); pmm = p.sem("pmm")
        pev = p.sem("pev"); rld = p.sem("rld"); sto = p.sem("sto")
        xod = p.sem("xod")

        for h in range(8):
            p.dma(nc.sync, qall[:, h * TOK:(h + 1) * TOK],
                  qt_in[h * 64:(h + 1) * 64, :], qld)
        src = wo_in[:, :].rearrange("(k r) c -> r k c", r=128)
        p.dma(nc.gpsimd, wosb[:, :].rearrange("r (k c) -> r k c", c=D), src, wld)
        p.dma(nc.gpsimd, wobb[:, :], wob_in[:, :], wld)
        for h in range(nh):
            bh = h % 2
            p.wait(nc.sync, smm)
            if kt_ranked:
                for rk in range(4):
                    p.dma(nc.sync, kh[bh][:, rk * TOK:(rk + 1) * TOK],
                          kt_in[rk * HEAD + h * 64:rk * HEAD + (h + 1) * 64, :], khl)
            else:
                p.dma(nc.sync, kh[bh][:, :], kt_in[h * 64:(h + 1) * 64, :], khl)
            p.wait(nc.sync, omm)
            p.dma(nc.sync, vh[bh][:, :].rearrange("r (k c) -> r k c", c=E),
                  v_in[:, h * 64:(h + 1) * 64].rearrange("(k r) c -> r k c", r=128), vhl)
            p.wait(nc.tensor, khl)
            p.wait(nc.tensor, qld)
            for t in range(ntt):
                for kc in range(SEQ // 512):
                    bank = ksc % 2
                    p.wait(nc.tensor, sev, slack=0 if ksc < 2 else 1)
                    ksc += 1
                    p.inc(nc.tensor.matmul(pS[bank][:, :],
                                           qall[:, h * TOK + t * 128:h * TOK + (t + 1) * 128],
                                           kh[bh][:, kc * 512:(kc + 1) * 512],
                                           start=True, stop=True), smm)
                    p.wait(nc.vector, smm)
                    p.inc(nc.vector.tensor_copy(ssb[:, kc * 512:(kc + 1) * 512],
                                                pS[bank][:, :]), sev)
                nc.vector.reduce_max(smx[:, 0:1], ssb[:, :], axis=AX)
                p.incd(nc.vector, sfx)
                nc.vector.tensor_scalar(ssb[:, :], ssb[:, :], smx[:, 0:1], None, ALU.subtract)
                p.incd(nc.vector, sfx)
                p.wait(nc.scalar, sfx)
                p.inc(nc.scalar.activation(ssb[:, :], ssb[:, :], AF.Exp), sfx)
                p.wait(nc.vector, sfx)
                nc.vector.reduce_sum(smx[:, 1:2], ssb[:, :], axis=AX)
                nc.vector.drain()
                nc.vector.reciprocal(smx[:, 2:3], smx[:, 1:2])
                nc.vector.drain()

                p.inc(nc.vector.tensor_scalar(asb[:, :], ssb[:, :], smx[:, 2:3],
                                              ISCL, ALU.mult, ALU.mult), sfx)
                p.wait(nc.tensor, sfx)
                p.wait(nc.tensor, vhl)
                for kc in range(SEQ // 128):
                    bk = ktc % 2
                    p.wait(nc.tensor, tev, slack=0 if ktc < 2 else 1)
                    ktc += 1
                    p.inc(nc.tensor.transpose(pTb[bk][:, :],
                                              asb[:, kc * 128:(kc + 1) * 128],
                                              ident[:, :]), trd)
                    p.wait(nc.vector, trd)
                    p.inc(nc.vector.tensor_copy(aTb[:, kc * 128:(kc + 1) * 128],
                                                pTb[bk][:, :]), tev)
                bo = it2 % 2
                p.wait(nc.tensor, tev)
                p.wait(nc.tensor, oev, slack=0 if it2 < 2 else 1)
                it2 += 1
                for kc in range(SEQ // 128):
                    mm = nc.tensor.matmul(pO[bo][:, :], aTb[:, kc * 128:(kc + 1) * 128],
                                          vh[bh][:, kc * E:(kc + 1) * E],
                                          start=(kc == 0), stop=(kc == SEQ // 128 - 1))
                p.inc(mm, omm)
                p.wait(nc.vector, omm)
                p.wait(nc.vector, xod)
                p.inc(nc.vector.tensor_copy(xo[:, t * HEAD + h * E:t * HEAD + (h + 1) * E],
                                            pO[bo][:, :]), oev)
        if not do_proj:
            p.wait(nc.sync, oev)
            p.wait(nc.sync, wld)
            return
        p.wait(nc.tensor, oev)
        p.wait(nc.tensor, wld)
        for t in range(NT):
            p.wait(nc.sync, pev)
            p.dma(nc.sync, resb[:, :], res_in[t * 128:(t + 1) * 128, :], rld)
            for k in range(4):
                p.wait(nc.tensor, xod, slack=0 if (t == 0 and k < 2) else 1)
                p.inc(nc.tensor.transpose(pTb[k % 2][:, :],
                                          xo[:, t * HEAD + k * 128:t * HEAD + (k + 1) * 128],
                                          ident[:, :]), trd)
                p.wait(nc.vector, trd)
                p.inc(nc.vector.tensor_copy(xoT[:, k * 128:(k + 1) * 128],
                                            pTb[k % 2][:, :]), xod)
            p.wait(nc.tensor, xod)
            for j in range(D // 512):
                bank = j % 2
                p.wait(nc.tensor, pev, slack=0 if (t * 8 + j) < 2 else 1)
                for k in range(4):
                    nc.tensor.matmul(pP[bank][:, :], xoT[:, k * 128:(k + 1) * 128],
                                     wosb[:, k * D + j * 512:k * D + (j + 1) * 512],
                                     start=(k == 0), stop=False)
                mm = nc.tensor.matmul(pP[bank][:, :], onesb[:, :],
                                      wobb[:, j * 512:(j + 1) * 512],
                                      start=False, stop=True)
                p.inc(mm, pmm)
                p.wait(nc.vector, pmm)
                p.wait(nc.vector, rld)
                p.wait(nc.vector, sto)
                p.inc(nc.vector.tensor_add(evac[:, j * 512:(j + 1) * 512],
                                           pP[bank][:, :],
                                           resb[:, j * 512:(j + 1) * 512]), pev)
            p.wait(nc.sync, pev)
            p.dma(nc.sync, out_o[t * 128:(t + 1) * 128, :], evac[:, :], sto)
        p.wait(nc.sync, sto)


def build_C2(nh=8, ntt=NT, do_proj=True):
    p = P("c2")
    nc = p.nc
    HEAD, SEQ = 512, 4096
    io = [nc.dram_tensor("qt", [HEAD, TOK], BF16, kind="ExternalInput"),
          nc.dram_tensor("ktf", [HEAD, SEQ], BF16, kind="ExternalInput"),
          nc.dram_tensor("vf", [SEQ, HEAD], BF16, kind="ExternalInput"),
          nc.dram_tensor("wo", [HEAD, D], F32, kind="ExternalInput"),
          nc.dram_tensor("wob", [1, D], F32, kind="ExternalInput"),
          nc.dram_tensor("res", [TOK, D], F32, kind="ExternalInput"),
          nc.dram_tensor("out", [TOK, D], F32, kind="ExternalOutput")]
    emit_C2(p, *io, kt_ranked=False, nh=nh, ntt=ntt, do_proj=do_proj)
    return p.nc


def build_fused():
    p = P("mk")
    nc = p.nc
    HEAD = 512
    x_in = nc.dram_tensor("x", [TOK, D], F32, kind="ExternalInput")
    xh_in = nc.dram_tensor("xh", [3, D], F32, kind="ExternalInput")
    w_ip = nc.dram_tensor("w_ip", [D, PROJ], F32, kind="ExternalInput")
    cw_in = nc.dram_tensor("cw", [CONV_DIM, 4], F32, kind="ExternalInput")
    cb_in = nc.dram_tensor("cb", [128, CONV_DIM // 128], F32, kind="ExternalInput")
    dtb_in = nc.dram_tensor("dtb", [128, 1], F32, kind="ExternalInput")
    al_in = nc.dram_tensor("al", [128, 1], F32, kind="ExternalInput")
    m8_in = nc.dram_tensor("mask8t", [4, 4], F32, kind="ExternalInput")
    mv_in = nc.dram_tensor("maskv", [4, 1], F32, kind="ExternalInput")
    sel_in = nc.dram_tensor("e8sel", [4, 4 * 128], F32, kind="ExternalInput")
    d_in = nc.dram_tensor("dvec", [1, H], F32, kind="ExternalInput")
    nw_in = nc.dram_tensor("nw", [128, INTER // 128], F32, kind="ExternalInput")
    op_in = nc.dram_tensor("opT", [INTER, D], F32, kind="ExternalInput")
    zb_in = nc.dram_tensor("zbias", [1, D], F32, kind="ExternalInput")
    qw_in = nc.dram_tensor("qw", [D, D], I32, kind="ExternalInput")
    sc_in = nc.dram_tensor("sc", [D, 32], F32, kind="ExternalInput")
    zr_in = nc.dram_tensor("zr", [D, 32], F32, kind="ExternalInput")
    fb_in = nc.dram_tensor("fbias", [1, D], F32, kind="ExternalInput")
    wq_in = nc.dram_tensor("wq", [D, HEAD], F32, kind="ExternalInput")
    wk_in = nc.dram_tensor("wk", [D, HEAD], F32, kind="ExternalInput")
    wv_in = nc.dram_tensor("wv", [D, HEAD], F32, kind="ExternalInput")
    qb_in = nc.dram_tensor("qb", [128, HEAD // 128], F32, kind="ExternalInput")
    kb_in = nc.dram_tensor("kb", [128, HEAD // 128], F32, kind="ExternalInput")
    vb_in = nc.dram_tensor("vb", [1, HEAD], F32, kind="ExternalInput")
    wo_in = nc.dram_tensor("woT", [HEAD, D], F32, kind="ExternalInput")
    wob_in = nc.dram_tensor("wob", [1, D], F32, kind="ExternalInput")
    w1_in = nc.dram_tensor("w1T", [D, D], F32, kind="ExternalInput")
    db_in = nc.dram_tensor("dbias", [1, D], F32, kind="ExternalInput")
    lsi_in = nc.dram_tensor("lsi", [1, 4], I32, kind="ExternalInput")
    bid_in = nc.dram_tensor("bid", [1, 1], F32, kind="ExternalInput")
    lat_in = nc.dram_tensor("lat", [3, D, 16], F32, kind="ExternalInput")
    lbt_in = nc.dram_tensor("lbt", [3, 16, D], F32, kind="ExternalInput")
    out_o = nc.dram_tensor("out", [TOK, D], F32, kind="ExternalOutput")
    gate_d = nc.dram_tensor("gate_d", [TOK, INTER], BF16)
    hbc_d = nc.dram_tensor("hbc_d", [CONV_DIM, TOK + 3], BF16)
    dt_d = nc.dram_tensor("dt_d", [H, TOK], F32)
    hs_d = nc.dram_tensor("hs_d", [TOK, INTER], BF16)
    ccm_d = nc.dram_tensor("ccm_d", [G * Nd, TOK], BF16)
    acs_d = nc.dram_tensor("acs_d", [TOK, H], F32)
    dpre_d = nc.dram_tensor("dpre_d", [128, NCHUNK * H], F32)
    logd_d = nc.dram_tensor("logd_d", [1, H], F32)
    e_d = nc.dram_tensor("e_d", [128, INTER], F32)
    eag_d = nc.dram_tensor("eag_d", [4 * 128, INTER], F32)
    logdag_d = nc.dram_tensor("logdag_d", [4, H], F32)
    ydiag_d = nc.dram_tensor("ydiag_d", [TOK, INTER], F32)
    pfx_d = nc.dram_tensor("pfx_d", [TOK, INTER], BF16)
    xn_d = nc.dram_tensor("xn_d", [TOK, INTER], BF16)
    x1_d = nc.dram_tensor("x1_d", [TOK, D], F32)
    x2_d = nc.dram_tensor("x2_d", [TOK, D], F32)
    x3_d = nc.dram_tensor("x3_d", [TOK, D], F32)
    wdq_d = nc.dram_tensor("wdq_d", [D, D], BF16)
    qt_d = nc.dram_tensor("qt_d", [HEAD, TOK], BF16)
    kt_d = nc.dram_tensor("kt_d", [HEAD, TOK], BF16)
    v_d = nc.dram_tensor("v_d", [TOK, HEAD], BF16)
    ktag_d = nc.dram_tensor("ktag_d", [4 * HEAD, TOK], BF16)
    vag_d = nc.dram_tensor("vag_d", [4 * TOK, HEAD], BF16)
    GRP = [[0, 1, 2, 3], [4, 5, 6, 7]]

    p.push("a1")
    emit_A1(p, x_in, w_ip, gate_d, hbc_d, dt_d, xh_in=xh_in)
    p.barrier()
    p.push("a2")
    emit_A2(p, hbc_d, dt_d, cw_in, cb_in, dtb_in, al_in, hs_d, ccm_d, acs_d,
            dpre_d, logd_d, e_d, pfx_d, ydiag_d)
    p.barrier()
    p.push("ag1")
    cg1 = p.sem("cg")
    p.inc(nc.gpsimd.collective_compute("AllGather", ALU.bypass, GRP,
                                       [e_d.ap().opt()], [eag_d.ap().opt()]), cg1)
    p.inc(nc.gpsimd.collective_compute("AllGather", ALU.bypass, GRP,
                                       [logd_d.ap().opt()], [logdag_d.ap().opt()]), cg1)
    for e in [nc.tensor, nc.vector, nc.scalar, nc.gpsimd, nc.sync]:
        p.wait(e, cg1)
    p.barrier()
    p.push("a3a")
    emit_A3a(p, ydiag_d, hs_d, gate_d, ccm_d, pfx_d, acs_d, dpre_d, eag_d,
             logdag_d, m8_in, mv_in, sel_in, d_in, xn_d, NW=4)
    p.barrier()
    p.push("a3b")
    emit_mm(p, INTER, D, False, False, False, True,
            dict(x=xn_d, res=x_in, bias=zb_in, w=op_in, nw=nw_in, out=x1_d), OFW=256)
    p.barrier()
    p.push("bffn")
    emit_mm(p, D, D, True, True, False, False,
            dict(x=x1_d, res=x1_d, bias=fb_in, qw=qw_in, sc=sc_in, zr=zr_in,
                 wdq=wdq_d, out=x2_d))
    p.barrier()
    p.push("c1")
    emit_C1(p, x2_d, wq_in, wk_in, wv_in, qb_in, kb_in, vb_in, qt_d, kt_d, v_d)
    p.barrier()
    p.push("ag2")
    cg2 = p.sem("cg")
    p.inc(nc.gpsimd.collective_compute("AllGather", ALU.bypass, GRP,
                                       [kt_d.ap().opt()], [ktag_d.ap().opt()]), cg2)
    p.inc(nc.gpsimd.collective_compute("AllGather", ALU.bypass, GRP,
                                       [v_d.ap().opt()], [vag_d.ap().opt()]), cg2)
    for e in [nc.tensor, nc.vector, nc.scalar, nc.gpsimd, nc.sync]:
        p.wait(e, cg2)
    p.barrier()
    p.push("c2")
    emit_C2(p, qt_d, ktag_d, vag_d, wo_in, wob_in, x2_d, x3_d, kt_ranked=True)
    p.barrier()
    p.push("dffn")
    emit_mm(p, D, D, True, False, True, False,
            dict(x=x3_d, res=x3_d, bias=db_in, lsi=lsi_in, bid=bid_in,
                 lat=lat_in, lbt=lbt_in, w=w1_in, out=out_o))
    p.barrier()
    p.gstack.close()
    return p.nc


def kernel_fused(**inputs):
    f32 = np.float32
    hid = np.asarray(inputs["L0_attn_hidden_states"], f32).reshape(8192, D)
    wT = _c(np.asarray(inputs["L0_attn_in_proj_w"], f32).T)
    conv_w = np.asarray(inputs["L0_attn_conv_w"], f32).reshape(CONV_DIM, 4)
    conv_bT = _c(np.asarray(inputs["L0_attn_conv_b"], f32).reshape(80, 128).T)
    dt_bias = _c(np.asarray(inputs["L0_attn_dt_bias"], f32).reshape(128, 1))
    A_log = _c(np.asarray(inputs["L0_attn_A_log"], f32).reshape(128, 1))
    Dv = _c(np.asarray(inputs["L0_attn_D"], f32).reshape(1, H))
    nwT = _c(np.asarray(inputs["L0_attn_norm_weight"], f32).reshape(INTER // 128, 128).T)
    opT = _c(np.asarray(inputs["L0_attn_out_proj_w"], f32).T)
    qw = _c(np.asarray(inputs["L0_ffn_qweight"]).reshape(D, D).astype(np.int32))
    sc = _c(np.asarray(inputs["L0_ffn_scales"], f32).reshape(D, 32))
    zr = _c(np.asarray(inputs["L0_ffn_scaled_zeros"], f32).reshape(D, 32))
    fb = _c(np.asarray(inputs["L0_ffn_bias"], f32).reshape(1, D))
    wqk = np.asarray(inputs["L1_attn_Wqk_w"], f32)
    wqT = _c(wqk[0::2].T)
    wkT = _c(wqk[1::2].T)
    bqk = np.asarray(inputs["L1_attn_Wqk_b"], f32)
    qbT = _c(bqk[0::2].reshape(4, 128).T)
    kbT = _c(bqk[1::2].reshape(4, 128).T)
    wvT = _c(np.asarray(inputs["L1_attn_Wv_w"], f32).T)
    vb = _c(np.asarray(inputs["L1_attn_Wv_b"], f32).reshape(1, 512))
    woT = _c(np.asarray(inputs["L1_attn_Wo_w"], f32).T)
    wob = _c(np.asarray(inputs["L1_attn_Wo_b"], f32).reshape(1, D))
    w1T = _c(np.asarray(inputs["L1_ffn_weight"], f32).T)
    db = _c(np.asarray(inputs["L1_ffn_bias"], f32).reshape(1, D))
    latT = _c(np.asarray(inputs["L1_ffn_lora_A"], f32).transpose(0, 2, 1))
    lbtT = _c(np.asarray(inputs["L1_ffn_lora_B"], f32).transpose(0, 2, 1))
    lsi = _c(np.asarray(inputs["L1_ffn_lora_start_indices"]).reshape(1, 4).astype(np.int32))
    zeros_bias = np.zeros((1, D), f32)
    e8sel = np.zeros((4, 4 * 128), f32)
    for q in range(4):
        e8sel[q, q * 128:(q + 1) * 128] = 1.0
    maps = []
    for c in range(NCORES):
        pos = c % 4
        m8 = np.zeros((4, 4), f32)
        mv = np.zeros((4, 1), f32)
        for q in range(4):
            if q < pos:
                mv[q, 0] = 1.0
                for r in range(4):
                    if q < r < pos:
                        m8[r, q] = 1.0
        xh = np.zeros((3, D), f32) if pos == 0 else _c(hid[c * TOK - 3:c * TOK])
        maps.append(dict(x=_c(hid[c * TOK:(c + 1) * TOK]), xh=xh, w_ip=wT, cw=conv_w,
                         cb=conv_bT, dtb=dt_bias, al=A_log, mask8t=m8, maskv=mv,
                         e8sel=e8sel, dvec=Dv, nw=nwT, opT=opT, zbias=zeros_bias,
                         qw=qw, sc=sc, zr=zr, fbias=fb, wq=wqT, wk=wkT, wv=wvT,
                         qb=qbT, kb=kbT, vb=vb, woT=woT, wob=wob, w1T=w1T, dbias=db,
                         lsi=lsi, bid=np.full((1, 1), float(c // 4), f32),
                         lat=latT, lbt=lbtT))
    res = run_bass_kernel_spmd(build_fused(), maps, core_ids=list(range(NCORES)))
    out = np.concatenate([np.asarray(r["out"]) for r in res.results], axis=0)
    return out.reshape(2, 4096, D)


# version 13 (final)
# speedup vs baseline: 1.1785x; 1.1785x over previous
import numpy as np
from contextlib import ExitStack

import concourse.bass as bass
import concourse.mybir as mybir
from concourse.bass_utils import run_bass_kernel_spmd

F32 = mybir.dt.float32
BF16 = mybir.dt.bfloat16
I32 = mybir.dt.int32
AF = mybir.ActivationFunctionType
ALU = mybir.AluOpType
AX = mybir.AxisListType.X

NCORES = 8
TOK = 1024
NT = TOK // 128
D = 4096
KD = D // 128
EPS = 1e-5

H, Pd, Nd, G, Kc, CH = 128, 64, 128, 8, 4, 128
INTER = H * Pd
CONV_DIM = INTER + 2 * G * Nd
PROJ = INTER + CONV_DIM + H
NCHUNK = TOK // CH


class P:

    def __init__(self, name):
        self.nc = bass.Bass("TRN2", target_bir_lowering=False, name=name)
        self.gstack = ExitStack()
        self._stack = self.gstack
        self.prefix = ""
        self.tot = {}
        self._semn = 0
        self._bar = None

    @property
    def stack(self):
        return self._stack

    def push(self, name):
        self.prefix = name + "_"
        self._stack = ExitStack()
        self._cc = {}
        return self._stack

    def barrier(self):
        nc = self.nc
        if self._bar is None:
            self.prefix, sv = "", self.prefix
            self._bar = self.sem("stagebar")
            self.prefix = sv
        engs = [nc.tensor, nc.vector, nc.scalar, nc.gpsimd, nc.sync]
        for e in engs:
            self.inc(e.drain(), self._bar)
        for e in engs:
            self.wait(e, self._bar)

    def sem(self, name=None):
        name = name or f"s{self._semn}"
        self._semn += 1
        if not hasattr(self, "_sems"):
            self._sems = {}
        if name not in self._sems:
            s = self.gstack.enter_context(self.nc.semaphore(name))
            self.tot[s.name] = 0
            self._sems[name] = s
        return self._sems[name]

    def inc(self, instr, sem, amt=1):
        instr.then_inc(sem, amt)
        self.tot[sem.name] += amt
        return instr

    def incd(self, eng, sem, amt=1):
        return self.inc(eng.drain(), sem, amt)

    def dma(self, q, dst, src, sem, **kw):
        ins = q.dma_start(dst, src, **kw)
        return self.inc(ins, sem, 16)

    def wait(self, eng, sem, slack=0):
        eng.wait_ge(sem, self.tot[sem.name] - slack)

    def sb(self, name, shape, dtype):
        return self._stack.enter_context(self.nc.sbuf_tensor(self.prefix + name, shape, dtype))

    def ps(self, name, shape, dtype=F32):
        return self._stack.enter_context(self.nc.psum_tensor(self.prefix + name, shape, dtype))

    def ccol(self, val):
        if not hasattr(self, "_cc"):
            self._cc = {}
        if not hasattr(self, "cst"):
            pre, self.prefix = self.prefix, ""
            self.cst = self.sem("cst")
            self.prefix = pre
        if val not in self._cc:
            t = self.sb(f"cc{len(self._cc)}", [128, 1], F32)
            self.inc(self.nc.gpsimd.memset(t[:, :], float(val)), self.cst)
            self._cc[val] = t
        return self._cc[val][:, 0:1]

    def wait_consts(self, *engs):
        if hasattr(self, "cst"):
            for e in engs:
                self.wait(e, self.cst)

    def make_identity(self, dtype=BF16):
        nc = self.nc
        ident = self.sb("identity", [128, 128], dtype)
        s = self.sem("ident")
        nc.gpsimd.memset(ident[:, :], 0.0)
        self.inc(nc.gpsimd.affine_select(ident[:, :], ident[:, :], [[-1, 128]],
                                         ALU.not_equal, 1.0, channel_multiplier=1), s)
        return ident, s


def ln_tile(p, x_f32, out_bf, scratch, s1, s2, ssq, svar, srstd, sdone):
    nc = p.nc
    nf = x_f32.shape[1]
    nc.vector.reduce_sum(s1[:, 0:1], x_f32[:, :], axis=AX)
    nc.vector.tensor_tensor(scratch[:, :], x_f32[:, :], x_f32[:, :], ALU.mult)
    nc.vector.reduce_sum(s1[:, 1:2], scratch[:, :], axis=AX)
    p.incd(nc.vector, ssq)
    nc.vector.drain()
    nc.vector.tensor_scalar(s2[:, 0:1], s1[:, 0:1], 1.0 / nf, None, ALU.mult)
    nc.vector.tensor_scalar(s2[:, 1:2], s1[:, 1:2], 1.0 / nf, None, ALU.mult)
    nc.vector.drain()
    nc.vector.tensor_tensor(s1[:, 0:1], s2[:, 0:1], s2[:, 0:1], ALU.mult)
    nc.vector.drain()
    nc.vector.tensor_sub(s2[:, 1:2], s2[:, 1:2], s1[:, 0:1])
    p.incd(nc.vector, svar)
    p.wait(nc.scalar, svar)
    b_eps, s_half = p.ccol(EPS), p.ccol(-0.5)
    p.wait_consts(nc.scalar)
    nc.scalar.activation(s1[:, 1:2], s2[:, 1:2], AF.Ln, bias=b_eps)
    nc.scalar.drain()
    nc.scalar.activation(s2[:, 1:2], s1[:, 1:2], AF.Exp, scale=s_half)
    p.incd(nc.scalar, srstd)
    p.wait(nc.vector, srstd)
    p.inc(nc.vector.tensor_scalar(out_bf[:, :], x_f32[:, :], s2[:, 0:1], s2[:, 1:2],
                                  ALU.subtract, ALU.mult), sdone)


def ln_and_transpose(p, x_in, xt, ident, ident_s, ntiles=NT):
    nc = p.nc
    nf = len(xt) * 128
    xrow = p.sb("ln_xrow", [128, nf], F32)
    p.ln_xrow = xrow
    xln = p.sb("ln_xln", [128, nf], BF16)
    scr = p.sb("ln_scr", [128, nf], F32)
    p.ln_scr = scr
    p.ln_xln = xln
    st1 = p.sb("ln_st1", [128, 2], F32)
    st2 = p.sb("ln_st2", [128, 2], F32)
    ld = p.sem("ln_ld")
    ssq, svar, srstd, sdone = (p.sem("ln_ssq"), p.sem("ln_svar"),
                               p.sem("ln_srstd"), p.sem("ln_sdone"))
    tr = p.sem("ln_tr")
    xtdone = p.sem("ln_xt")
    p.wait(nc.tensor, ident_s)
    it = 0
    for t in range(ntiles):
        p.wait(nc.sync, sdone)
        p.wait(nc.sync, ssq)
        p.dma(nc.sync, xrow[:, :], x_in[t * 128:(t + 1) * 128, :], ld)
        p.wait(nc.vector, ld)
        p.wait(nc.scalar, ld)
        ln_tile(p, xrow, xln, scr, st1, st2, ssq, svar, srstd, sdone)
        p.wait(nc.tensor, sdone)
        for k in range(len(xt)):
            psl = p.pst_ln[it % 2][:, :]
            p.wait(nc.tensor, xtdone, slack=0 if it < 2 else 1)
            p.inc(nc.tensor.transpose(psl, xln[:, k * 128:(k + 1) * 128],
                                      ident[:, :]), tr)
            p.wait(nc.vector, tr)
            p.inc(nc.vector.tensor_copy(xt[k][:, t * 128:(t + 1) * 128], psl), xtdone)
            it += 1
    return xtdone


def load_and_transpose(p, x_in, xt, ident, ident_s, scale_sb=None, ntiles=NT):
    nc = p.nc
    nf = len(xt) * 128
    xrow = p.sb("lt_xrow", [128, nf], BF16)
    ld = p.sem("lt_ld")
    tr = p.sem("lt_tr")
    xtdone = p.sem("lt_xt")
    p.wait(nc.tensor, ident_s)
    it = 0
    for t in range(ntiles):
        p.wait(nc.sync, xtdone)
        p.dma(nc.sync, xrow[:, :], x_in[t * 128:(t + 1) * 128, :], ld)
        p.wait(nc.tensor, ld)
        for k in range(len(xt)):
            psl = p.pst_ln[it % 2][:, :]
            it += 1
            p.wait(nc.tensor, xtdone, slack=0 if it <= 2 else 1)
            p.inc(nc.tensor.transpose(psl, xrow[:, k * 128:(k + 1) * 128],
                                      ident[:, :]), tr)
            p.wait(nc.vector, tr)
            if scale_sb is not None:
                ins = nc.vector.tensor_scalar(xt[k][:, t * 128:(t + 1) * 128],
                                              psl, scale_sb[:, k:k + 1], None, ALU.mult)
            else:
                ins = nc.vector.tensor_copy(xt[k][:, t * 128:(t + 1) * 128], psl)
            p.inc(ins, xtdone)
    return xtdone


def emit_A1(p, x_in, w_in, gate_o, hbc_o, dt_o, xh_in=None):
    nc = p.nc
    hof = 3 if xh_in is not None else 0
    with p.stack:
        ident, ident_s = p.make_identity(BF16)
        p.pst_ln = [p.ps("pst_ln0", [128, 128], BF16), p.ps("pst_ln1", [128, 128], BF16)]
        xt = [p.sb(f"xt{k}", [128, TOK], BF16) for k in range(KD)]
        xtdone = ln_and_transpose(p, x_in, xt, ident, ident_s)
        if xh_in is not None:
            xhr = p.ln_xrow
            xhl = p.ln_xln
            hscr = p.ln_scr
            hs1 = p.sb("hs1", [128, 2], F32)
            hs2 = p.sb("hs2", [128, 2], F32)
            xthb = p.sb("xthb", [128, KD * 4], BF16)
            hld_ = p.sem("hld_")
            hssq, hsvar, hsrstd, hsdone = (p.sem("h_ssq"), p.sem("h_svar"),
                                           p.sem("h_srstd"), p.sem("h_sdone"))
            htr = p.sem("htr")
            hxt = p.sem("hxt")
            p.wait(nc.vector, xtdone)
            nc.vector.memset(xhr[:, :], 0.0)
            p.incd(nc.vector, hld_)
            p.wait(nc.sync, hld_)
            p.dma(nc.sync, xhr[0:3, :].rearrange("a b -> a b"), xh_in[:, :], hld_)
            p.wait(nc.vector, hld_)
            p.wait(nc.scalar, hld_)
            ln_tile(p, xhr, xhl, hscr, hs1, hs2, hssq, hsvar, hsrstd, hsdone)
            p.wait(nc.tensor, hsdone)
            p.wait(nc.tensor, xtdone)
            for k in range(KD):
                p.wait(nc.tensor, hxt)
                p.inc(nc.tensor.transpose(p.pst_ln[0][:, :], xhl[:, k * 128:(k + 1) * 128],
                                          ident[:, :]), htr)
                p.wait(nc.vector, htr)
                p.inc(nc.vector.tensor_copy(xthb[:, k * 4:k * 4 + 3], p.pst_ln[0][:, 0:3]),
                      hxt)

        wbuf = [p.sb(f"wbuf{i}", [128, KD * 512], BF16) for i in range(2)]
        evac = p.sb("evac", [128, NT * 512], BF16)
        hevac = p.sb("hevac", [128, TOK], BF16)
        hevac2 = p.sb("hevac2", [128, 4], BF16)
        devac = p.sb("devac", [128, TOK], F32)
        psg = [p.ps(f"psg{i}", [128, 512], F32) for i in range(4)]

        wl = [p.sem(f"wl{i}") for i in range(2)]
        mmd = p.sem("mmd")
        evs = p.sem("evs")
        sto = p.sem("sto")

        p.wait(nc.tensor, xtdone)

        NOF_G = INTER // 512
        for j in range(NOF_G):
            b = j % 2
            p.wait(nc.gpsimd, mmd, slack=0 if j < 2 else NT)
            src = w_in[:, j * 512:(j + 1) * 512].rearrange("(k r) c -> r k c", r=128)
            p.dma(nc.gpsimd, wbuf[b][:, :].rearrange("r (k c) -> r k c", c=512), src, wl[b])
            p.wait(nc.tensor, wl[b])
            for t in range(NT):
                bank = t % 4
                p.wait(nc.tensor, evs, slack=0 if (j == 0 and t < 4) else 3)
                for k in range(KD):
                    mm = nc.tensor.matmul(psg[bank][:, :], xt[k][:, t * 128:(t + 1) * 128],
                                          wbuf[b][:, k * 512:(k + 1) * 512],
                                          start=(k == 0), stop=(k == KD - 1))
                p.inc(mm, mmd)
                p.wait(nc.vector, mmd)
                p.wait(nc.vector, sto)
                p.inc(nc.vector.tensor_copy(evac[:, t * 512:(t + 1) * 512], psg[bank][:, :]),
                      evs)
            p.wait(nc.sync, evs)
            dst = gate_o[:, j * 512:(j + 1) * 512].rearrange("(u r) c -> r u c", r=128)
            p.dma(nc.sync, dst, evac[:, :].rearrange("r (u c) -> r u c", c=512), sto)

        NCH = CONV_DIM // 128
        for j in range(NCH + 1):
            b = j % 2
            isdt = (j == NCH)
            col0 = INTER + j * 128
            p.wait(nc.gpsimd, mmd, slack=0 if j < 2 else 2)
            src = w_in[:, col0:col0 + 128].rearrange("(k r) c -> r k c", r=128)
            p.dma(nc.gpsimd, wbuf[b][:, 0:KD * 128].rearrange("r (k c) -> r k c", c=128),
                  src, wl[b])
            p.wait(nc.tensor, wl[b])
            for half in range(2):
                bank = half % 4
                p.wait(nc.tensor, evs, slack=0 if j == 0 else 1)
                for k in range(KD):
                    mm = nc.tensor.matmul(psg[bank][:, :], wbuf[b][:, k * 128:(k + 1) * 128],
                                          xt[k][:, half * 512:(half + 1) * 512],
                                          start=(k == 0), stop=(k == KD - 1))
                p.inc(mm, mmd)
                p.wait(nc.vector, mmd)
                p.wait(nc.vector, sto)
                dstb = devac if isdt else hevac
                p.inc(nc.vector.tensor_copy(dstb[:, half * 512:(half + 1) * 512],
                                            psg[bank][:, :]), evs)
            if xh_in is not None and not isdt:
                p.wait(nc.tensor, evs, slack=0 if j == 0 else 1)
                p.wait(nc.tensor, hxt)
                for k in range(KD):
                    mm = nc.tensor.matmul(psg[2][:, 0:3], wbuf[b][:, k * 128:(k + 1) * 128],
                                          xthb[:, k * 4:k * 4 + 3],
                                          start=(k == 0), stop=(k == KD - 1))
                p.inc(mm, mmd)
                p.wait(nc.vector, mmd)
                p.inc(nc.vector.tensor_copy(hevac2[:, 0:3], psg[2][:, 0:3]), evs)
            p.wait(nc.sync, evs)
            if isdt:
                p.dma(nc.sync, dt_o[:, :], devac[:, :], sto)
            else:
                if xh_in is not None:
                    p.dma(nc.sync, hbc_o[j * 128:(j + 1) * 128, 0:3], hevac2[:, 0:3], sto)
                p.dma(nc.sync, hbc_o[j * 128:(j + 1) * 128, hof:hof + TOK], hevac[:, :], sto)
        p.wait(nc.sync, sto)


def build_A1():
    p = P("a1")
    nc = p.nc
    x_in = nc.dram_tensor("x", [TOK, D], F32, kind="ExternalInput")
    w_in = nc.dram_tensor("w", [D, PROJ], F32, kind="ExternalInput")
    gate_o = nc.dram_tensor("gate", [TOK, INTER], BF16, kind="ExternalOutput")
    hbc_o = nc.dram_tensor("hbc", [CONV_DIM, TOK], BF16, kind="ExternalOutput")
    dt_o = nc.dram_tensor("dt", [H, TOK], F32, kind="ExternalOutput")
    emit_A1(p, x_in, w_in, gate_o, hbc_o, dt_o)
    return p.nc


def emit_A2(p, hbc_in, dt_in, cw_in, cb_in, dtb_in, al_in, hs_o, ccm_o, acs_o,
            dpre_o, logd_o, e_o, pfx_o, yd_o, phases="cds"):
    nc = p.nc
    with p.stack:
        identb, ident_s = p.make_identity(BF16)
        identf = p.sb("identf", [128, 128], F32)
        nc.gpsimd.memset(identf[:, :], 0.0)
        p.inc(nc.gpsimd.affine_select(identf[:, :], identf[:, :], [[-1, 128]],
                                      ALU.not_equal, 1.0, channel_multiplier=1), ident_s)
        uincl = p.sb("uincl", [128, 128], F32)
        nc.gpsimd.memset(uincl[:, :], 1.0)
        p.inc(nc.gpsimd.affine_select(uincl[:, :], uincl[:, :], [[1, 128]],
                                      ALU.is_ge, 0.0, channel_multiplier=-1), ident_s)
        su = p.sb("su", [128, 128], F32)
        nc.gpsimd.memset(su[:, :], 1.0)
        p.inc(nc.gpsimd.affine_select(su[:, :], su[:, :], [[-1, 128]],
                                      ALU.is_gt, 0.0, channel_multiplier=1), ident_s)
        maskup = p.sb("maskup", [128, 128], F32)
        nc.gpsimd.memset(maskup[:, :], 1.0)
        p.inc(nc.gpsimd.affine_select(maskup[:, :], maskup[:, :], [[1, 128]],
                                      ALU.is_ge, 0.0, channel_multiplier=-1), ident_s)
        onesf = p.sb("onesf", [128, 128], F32)
        p.inc(nc.gpsimd.memset(onesf[:, :], 1.0), ident_s)
        cm1, cp1 = p.ccol(-1.0), p.ccol(1.0)

        pGm = p.ps("pGm", [128, 512], F32)
        pAcb = pGm.bitcast(BF16)
        pA = pGm
        pSt = [p.ps(f"pSt{i}", [128, 512], F32) for i in range(3)]
        pY = [p.ps(f"pY{i}", [128, 512], F32) for i in range(2)]
        pSn = [p.ps(f"pSn{i}", [128, 512], F32) for i in range(2)]
        pAb = pA[:, 0:128]
        pAc = pAcb[:, 0:128]

        btm = [p.sb(f"btm{g}", [128, NCHUNK * 128], BF16) for g in range(G)]
        bcm = [p.sb(f"bcm{g}", [128, TOK], BF16) for g in range(G)]
        ccm = [p.sb(f"ccm{g}", [128, TOK], BF16) for g in range(G)]
        xsb = p.sb("xsb", [128, INTER], BF16)
        prev = p.sb("prev", [128, INTER], F32)
        ystage = p.sb("ystage", [128, 2048], F32)

        hbct = [p.sb(f"hbct{i}", [128, TOK + 3], BF16) for i in range(2)]
        cacc = [p.sb(f"cacc{i}", [128, TOK], F32) for i in range(2)]
        cwt = [p.sb(f"cwt{i}", [128, 4], F32) for i in range(2)]
        cbt = p.sb("cbt", [128, CONV_DIM // 128], F32)
        chbuf = [p.sb(f"chbuf{i}", [128, TOK], BF16) for i in range(2)]
        htst = p.sb("htst", [128, TOK], BF16)

        dts = [p.sb(f"dts{i}", [128, TOK], F32) for i in range(3)]
        dtsp = p.sb("dtsp", [128, TOK], F32)
        adt = p.sb("adt", [128, TOK], F32)
        dtbc = p.sb("dtbc", [128, 1], F32)
        alc = p.sb("alc", [128, 1], F32)
        acol = p.sb("acol", [128, 1], F32)
        adttm = p.sb("adttm", [128, TOK], F32)
        dttm = p.sb("dttm", [128, TOK], F32)
        acssb = p.sb("acssb", [128, TOK], F32)
        tbcsb = p.sb("tbcsb", [128, TOK], F32)
        dssb = p.sb("dssb", [128, TOK], F32)
        expt = p.sb("expt", [128, TOK], F32)
        dpres = p.sb("dpres", [128, TOK], F32)
        rsum = p.sb("rsum", [128, 128], F32)

        gmm = p.sb("gmm", [128, G * 128], F32)
        w2 = [p.sb(f"w2_{i}", [128, 512], F32) for i in range(4)]
        exps = [p.sb(f"expS{i}", [128, 512], F32) for i in range(4)]
        mst = [p.sb(f"mst{i}", [128, 512], BF16) for i in range(4)]
        bdb = [p.sb(f"bdb{i}", [128, 512], BF16) for i in range(4)]

        ld = p.sem("ld"); hld = p.sem("hld"); cvd = p.sem("cvd")
        trd = p.sem("trd"); evd = p.sem("evd"); hsto = p.sem("hsto")
        dtd = p.sem("dtd"); xld = p.sem("xld"); xbd = p.sem("xbd")
        gmd = p.sem("gmd"); gme = p.sem("gme")
        w2d = p.sem("w2d"); std = p.sem("std"); expd = p.sem("expd")
        msd = p.sem("msd"); ymd = p.sem("ymd"); yed = p.sem("yed")
        ysto = p.sem("ysto"); bdd = p.sem("bdd"); snd = p.sem("snd")
        pvd = p.sem("pvd"); pfc = p.sem("pfc"); pfs = p.sem("pfs")
        esto = p.sem("esto")

        p.dma(nc.sync, cbt[:, :], cb_in[:, :], ld)
        p.dma(nc.sync, dtbc[:, :], dtb_in[:, :], ld)
        p.dma(nc.sync, alc[:, :], al_in[:, :], ld)

        NCH = CONV_DIM // 128
        htst2 = [htst, p.sb("htstB", [128, TOK], BF16)]
        silu_after, cons_after, store_after = [], [], []
        for j in range(NCH):
            b = j % 2
            if j >= 2:
                nc.sync.wait_ge(cvd, silu_after[j - 2])
            p.dma(nc.sync, hbct[b][:, :], hbc_in[j * 128:(j + 1) * 128, :], hld)
            p.dma(nc.sync, cwt[b][:, :], cw_in[j * 128:(j + 1) * 128, :], hld)
            p.wait(nc.vector, hld)
            if j >= 2:
                nc.vector.wait_ge(cvd, silu_after[j - 2])
            nc.vector.tensor_scalar(cacc[b][:, :], hbct[b][:, 0:TOK], cwt[b][:, 0:1],
                                    None, ALU.mult)
            for k in range(1, 4):
                ins = nc.vector.scalar_tensor_tensor(cacc[b][:, :], hbct[b][:, k:k + TOK],
                                                     cwt[b][:, k:k + 1], cacc[b][:, :],
                                                     ALU.mult, ALU.add)
            p.inc(ins, cvd)
            p.wait(nc.scalar, cvd)
            p.wait(nc.scalar, ld)
            if j >= 2:
                nc.scalar.wait_ge(evd, cons_after[j - 2])
                nc.scalar.wait_ge(hsto, store_after[j - 2])
            p.inc(nc.scalar.activation(chbuf[b][:, :], cacc[b][:, :], AF.Silu,
                                       bias=cbt[:, j:j + 1]), cvd)
            silu_after.append(p.tot[cvd.name])
            if j < 72:
                p.wait(nc.tensor, cvd)
                p.wait(nc.tensor, ident_s)
                for tb in range(NT):
                    pslot = pAc
                    p.wait(nc.tensor, evd)
                    p.inc(nc.tensor.transpose(pslot, chbuf[b][:, tb * 128:(tb + 1) * 128],
                                              identb[:, :]), trd)
                    p.wait(nc.vector, trd)
                    if j < 64:
                        if j >= 2:
                            nc.vector.wait_ge(hsto, store_after[j - 2])
                        dstt = htst2[b][:, tb * 128:(tb + 1) * 128]
                    else:
                        dstt = btm[j - 64][:, tb * 128:(tb + 1) * 128]
                    p.inc(nc.vector.tensor_copy(dstt, pslot), evd)
                if j < 64:
                    p.wait(nc.sync, evd)
                    dst = hs_o[:, j * 128:(j + 1) * 128].rearrange("(tb q) c -> q tb c", q=128)
                    p.dma(nc.sync, dst, htst2[b][:, :].rearrange("q (tb c) -> q tb c", c=128), hsto)
                else:
                    p.wait(nc.gpsimd, cvd)
                    p.inc(nc.gpsimd.tensor_scalar(bcm[j - 64][:, :], chbuf[b][:, :],
                                                  1.0, None, ALU.mult), evd)
            else:
                gc = j - 72
                p.wait(nc.gpsimd, cvd)
                p.inc(nc.gpsimd.tensor_scalar(ccm[gc][:, :], chbuf[b][:, :],
                                              1.0, None, ALU.mult), evd)
                p.wait(nc.sync, cvd)
                p.dma(nc.sync, ccm_o[gc * 128:(gc + 1) * 128, :], chbuf[b][:, :], hsto)
            cons_after.append(p.tot[evd.name])
            store_after.append(p.tot[hsto.name])

        if "d" not in phases:
            p.wait(nc.sync, hsto)
            p.wait(nc.sync, evd)
            return
        p.dma(nc.sync, dts[2][:, :], dt_in[:, :], ld)
        p.wait(nc.vector, ld)
        nc.vector.tensor_scalar(dts[0][:, :], dts[2][:, :], dtbc[:, 0:1], None, ALU.add)
        p.incd(nc.vector, dtd)
        p.wait(nc.scalar, dtd)
        p.wait_consts(nc.scalar)
        nc.scalar.activation(dts[1][:, :], dts[0][:, :], AF.Abs)
        nc.scalar.activation(dts[1][:, :], dts[1][:, :], AF.Exp, scale=cm1)
        p.inc(nc.scalar.activation(dts[1][:, :], dts[1][:, :], AF.Ln, bias=cp1), dtd)
        nc.vector.tensor_scalar_max(dts[2][:, :], dts[0][:, :], 0.0)
        p.wait(nc.vector, dtd)
        nc.vector.tensor_add(dtsp[:, :], dts[2][:, :], dts[1][:, :])
        p.incd(nc.vector, dtd)
        p.wait(nc.scalar, dtd)
        nc.scalar.activation(acol[:, 0:1], alc[:, 0:1], AF.Exp)
        p.incd(nc.scalar, dtd)
        p.wait(nc.vector, dtd)
        nc.vector.tensor_scalar(acol[:, 0:1], acol[:, 0:1], -1.0, None, ALU.mult)
        nc.vector.drain()
        nc.vector.tensor_scalar(adt[:, :], dtsp[:, :], acol[:, 0:1], None, ALU.mult)
        nc.vector.memset(rsum[:, :], 0.0)
        p.incd(nc.vector, dtd)
        for c in range(NCHUNK):
            cc = c * 128
            p.wait(nc.tensor, dtd)
            p.inc(nc.tensor.transpose(pAb, adt[:, cc:cc + 128], identf[:, :]), trd)
            p.wait(nc.vector, trd)
            p.inc(nc.vector.tensor_copy(adttm[:, cc:cc + 128], pAb), dtd)
            p.wait(nc.tensor, dtd)
            p.inc(nc.tensor.transpose(pAb, dtsp[:, cc:cc + 128], identf[:, :]), trd)
            p.wait(nc.vector, trd)
            p.inc(nc.vector.tensor_copy(dttm[:, cc:cc + 128], pAb), dtd)
            p.wait(nc.tensor, dtd)
            p.inc(nc.tensor.matmul(pGm[:, 0:128], uincl[:, :], adttm[:, cc:cc + 128],
                                   start=True, stop=True), trd)
            p.inc(nc.tensor.matmul(pGm[:, 128:256], onesf[:, :], adttm[:, cc:cc + 128],
                                   start=True, stop=True), trd)
            p.wait(nc.vector, trd)
            nc.vector.tensor_copy(acssb[:, cc:cc + 128], pGm[:, 0:128])
            nc.vector.tensor_copy(tbcsb[:, cc:cc + 128], pGm[:, 128:256])
            nc.vector.tensor_sub(dssb[:, cc:cc + 128], tbcsb[:, cc:cc + 128],
                                 acssb[:, cc:cc + 128])
            p.incd(nc.vector, dtd)
            p.wait(nc.scalar, dtd)
            nc.scalar.activation(dssb[:, cc:cc + 128], dssb[:, cc:cc + 128], AF.Exp)
            nc.scalar.activation(expt[:, cc:cc + 128], tbcsb[:, cc:cc + 128], AF.Exp)
            p.inc(nc.scalar.activation(dpres[:, cc:cc + 128], rsum[:, :], AF.Exp), dtd)
            p.wait(nc.vector, dtd)
            p.inc(nc.vector.tensor_add(rsum[:, :], rsum[:, :], tbcsb[:, cc:cc + 128]), dtd)
        p.incd(nc.vector, dtd)
        p.wait(nc.sync, dtd)
        p.dma(nc.sync, acs_o[:, :].rearrange("(c l) h -> l c h", l=128),
              acssb[:, :].rearrange("l (c h) -> l c h", h=128), esto)
        p.dma(nc.sync, dpre_o[:, :], dpres[:, :], esto)
        p.dma(nc.sync, logd_o[:, :], rsum[0:1, :], esto)

        if "s" not in phases:
            p.wait(nc.sync, hsto)
            p.wait(nc.sync, esto)
            return
        nc.vector.memset(prev[:, :], 0.0)
        p.incd(nc.vector, pvd)
        p.wait(nc.vector, dtd)
        p.wait(nc.vector, evd)
        p.wait(nc.tensor, evd)
        NIT = NCHUNK * 32

        def emit_w2st(it):
            c2_, hb2 = divmod(it, 32)
            h2 = hb2 * 4
            b4 = it % 4
            p.wait(nc.vector, std, slack=0 if it < 4 else 3)
            for i in range(4):
                ins = nc.vector.tensor_scalar(
                    w2[b4][:, i * 128:(i + 1) * 128], uincl[:, :],
                    adttm[:, c2_ * 128 + h2 + i:c2_ * 128 + h2 + i + 1], None, ALU.mult)
            p.inc(ins, w2d)
            p.wait(nc.tensor, w2d)
            if it >= 3:
                nc.tensor.wait_ge(expd, it - 2)
            p.inc(nc.tensor.matmul(pSt[it % 3][:, :], su[:, :], w2[b4][:, :],
                                   start=True, stop=True), std)

        it = 0
        for c in range(NCHUNK):
            cc = c * 128
            p.wait(nc.sync, hsto)
            p.wait(nc.sync, ymd)
            p.wait(nc.sync, snd)
            p.dma(nc.sync, xsb[:, :], hs_o[cc:cc + 128, :], xld)
            p.wait(nc.gpsimd, pvd)
            p.dma(nc.gpsimd, pfx_o[cc:cc + 128, :], prev[:, :], pfs)
            p.wait(nc.vector, xld)
            dtap = dttm[:, cc:cc + 128].unsqueeze(2).broadcast_to((128, 128, Pd))
            p.inc(nc.vector.tensor_tensor(xsb[:, :].rearrange("l (h q) -> l h q", q=Pd),
                                          xsb[:, :].rearrange("l (h q) -> l h q", q=Pd),
                                          dtap, ALU.mult), xbd)
            for g in range(G):
                p.wait(nc.tensor, gme)
                p.wait(nc.tensor, dtd)
                p.inc(nc.tensor.matmul(pGm[:, 0:128], bcm[g][:, cc:cc + 128],
                                       ccm[g][:, cc:cc + 128], start=True, stop=True), gmd)
                p.wait(nc.vector, gmd)
                p.inc(nc.vector.tensor_tensor(gmm[:, g * 128:(g + 1) * 128], pGm[:, 0:128],
                                              maskup[:, :], ALU.mult), gme)
            p.wait(nc.tensor, xbd)
            p.wait(nc.tensor, gme)
            p.wait(nc.vector, pfs)
            for hb in range(32):
                h0 = hb * 4
                g0m = h0 % G
                b2 = it % 2
                b4 = it % 4
                if it == 0:
                    emit_w2st(0)
                    emit_w2st(1)
                if it + 2 < NIT:
                    emit_w2st(it + 2)
                nc.scalar.wait_ge(std, it + 1)
                p.inc(nc.scalar.activation(exps[b4][:, :], pSt[it % 3][:, :], AF.Exp),
                      expd)
                p.wait(nc.vector, expd)
                p.wait(nc.vector, ymd, slack=0 if it < 4 else 12)
                p.inc(nc.vector.tensor_tensor(
                    mst[b4][:, :], exps[b4][:, :],
                    gmm[:, g0m * 128:(g0m + 4) * 128], ALU.mult), msd)
                p.wait(nc.tensor, msd)
                p.wait(nc.tensor, yed, slack=0 if it < 2 else 1)
                for i in range(4):
                    mm = nc.tensor.matmul(pY[b2][:, i * 64:(i + 1) * 64],
                                          mst[b4][:, i * 128:(i + 1) * 128],
                                          xsb[:, (h0 + i) * Pd:(h0 + i + 1) * Pd],
                                          start=True, stop=True)
                    p.inc(mm, ymd)
                p.wait(nc.vector, snd, slack=0 if it < 4 else 12)
                for i in range(4):
                    ins = nc.vector.tensor_scalar(
                        bdb[b4][:, i * 128:(i + 1) * 128],
                        btm[(h0 + i) % G][:, cc:cc + 128],
                        dssb[:, cc + h0 + i:cc + h0 + i + 1], None, ALU.mult)
                p.inc(ins, bdd)
                p.wait(nc.tensor, bdd)
                p.wait(nc.tensor, pvd, slack=0 if it < 2 else 2)
                for i in range(4):
                    mm = nc.tensor.matmul(pSn[b2][:, i * 64:(i + 1) * 64],
                                          bdb[b4][:, i * 128:(i + 1) * 128],
                                          xsb[:, (h0 + i) * Pd:(h0 + i + 1) * Pd],
                                          start=True, stop=True)
                    p.inc(mm, snd)
                p.wait(nc.vector, ymd)
                if hb % 8 == 0:
                    p.wait(nc.vector, ysto)
                p.inc(nc.vector.tensor_copy(ystage[:, (hb % 8) * 256:(hb % 8 + 1) * 256],
                                            pY[b2][:, 0:256]), yed)
                if hb % 8 == 7:
                    p.wait(nc.sync, yed)
                    p.dma(nc.sync, yd_o[cc:cc + 128, (hb - 7) * 256:(hb + 1) * 256],
                          ystage[:, :], ysto)
                p.wait(nc.vector, snd)
                for i in range(4):
                    ins = nc.vector.scalar_tensor_tensor(
                        prev[:, (h0 + i) * Pd:(h0 + i + 1) * Pd],
                        prev[:, (h0 + i) * Pd:(h0 + i + 1) * Pd],
                        expt[:, cc + h0 + i:cc + h0 + i + 1],
                        pSn[b2][:, i * 64:(i + 1) * 64],
                        ALU.mult, ALU.add)
                p.inc(ins, pvd, 2)
                it += 1
                p.wait(nc.sync, pvd)
        p.dma(nc.sync, e_o[:, :], prev[:, :], esto)
        p.wait(nc.sync, esto)
        p.wait(nc.sync, ysto)
        p.wait(nc.sync, pfs)


def build_A2(phases="cds"):
    p = P("a2")
    nc = p.nc
    hbc_in = nc.dram_tensor("hbc", [CONV_DIM, TOK + 3], BF16, kind="ExternalInput")
    dt_in = nc.dram_tensor("dt", [H, TOK], F32, kind="ExternalInput")
    cw_in = nc.dram_tensor("cw", [CONV_DIM, 4], F32, kind="ExternalInput")
    cb_in = nc.dram_tensor("cb", [128, CONV_DIM // 128], F32, kind="ExternalInput")
    dtb_in = nc.dram_tensor("dtb", [128, 1], F32, kind="ExternalInput")
    al_in = nc.dram_tensor("al", [128, 1], F32, kind="ExternalInput")
    hs_o = nc.dram_tensor("hs", [TOK, INTER], BF16, kind="ExternalOutput")
    ccm_o = nc.dram_tensor("ccm", [G * Nd, TOK], BF16, kind="ExternalOutput")
    acs_o = nc.dram_tensor("acs", [TOK, H], F32, kind="ExternalOutput")
    dpre_o = nc.dram_tensor("dpre", [128, NCHUNK * H], F32, kind="ExternalOutput")
    logd_o = nc.dram_tensor("logd", [1, H], F32, kind="ExternalOutput")
    e_o = nc.dram_tensor("eout", [128, INTER], F32, kind="ExternalOutput")
    pfx_o = nc.dram_tensor("pfx", [NCHUNK * 128, INTER], BF16, kind="ExternalOutput")
    yd_o = nc.dram_tensor("ydiag", [TOK, INTER], F32, kind="ExternalOutput")
    emit_A2(p, hbc_in, dt_in, cw_in, cb_in, dtb_in, al_in, hs_o, ccm_o, acs_o,
            dpre_o, logd_o, e_o, pfx_o, yd_o, phases)
    return p.nc


def emit_mm(p, IN, OUT, ln, dequant, lora, scale_in, io, OFW=512):
    nc = p.nc
    KIN = IN // 128
    x_in = io["x"]; res_in = io["res"]; bias_in = io["bias"]; out_o = io["out"]
    if dequant:
        qw_in = io["qw"]; sc_in = io["sc"]; zr_in = io["zr"]; wdq_d = io["wdq"]
    else:
        w_in = io["w"]
    if scale_in:
        nw_in = io["nw"]
    if lora:
        lsi_in = io["lsi"]; bid_in = io["bid"]; lat_in = io["lat"]; lbt_in = io["lbt"]
    with p.stack:
        ident, ident_s = p.make_identity(BF16)
        p.pst_ln = [p.ps("pst_ln0", [128, 128], BF16), p.ps("pst_ln1", [128, 128], BF16)]
        xt = [p.sb(f"xt{k}", [128, TOK], BF16) for k in range(KIN)]
        onesb = p.sb("onesb", [1, 128], BF16)
        p.inc(nc.gpsimd.memset(onesb[:, :], 1.0), ident_s)
        NOF = OUT // OFW
        NWB = 1 if IN >= 8192 else 2
        wbuf = [p.sb(f"wbuf{i}", [128, KIN * OFW], BF16) for i in range(NWB)]
        evac = p.sb("evac", [128, 2 * OFW], F32)
        resb = p.sb("resb", [128, 2 * OFW], F32)
        biasb = p.sb("biasb", [1, OUT], BF16)
        psg = [p.ps(f"psg{i}", [128, 512], F32) for i in range(4)]

        wl = [p.sem(f"wl{i}") for i in range(2)]
        mmd = p.sem("mmd")
        evs = p.sem("evs")
        sto = p.sem("sto")
        rld = p.sem("rld")
        bld = p.sem("bld")
        p.dma(nc.gpsimd, biasb[:, :], bias_in[:, :], bld)

        if ln:
            xtdone = ln_and_transpose(p, x_in, xt, ident, ident_s)
        else:
            if scale_in:
                nwb = p.sb("nwb", [128, KIN], F32)
                p.dma(nc.sync, nwb[:, :], nw_in[:, :], bld)
                p.wait(nc.vector, bld)
                xtdone = load_and_transpose(p, x_in, xt, ident, ident_s, scale_sb=nwb)
            else:
                xtdone = load_and_transpose(p, x_in, xt, ident, ident_s)

        if lora:
            lsb = p.sb("lsb", [1, 4], I32)
            lsf = p.sb("lsf", [1, 16], F32)
            coef = p.sb("coef", [1, 4], F32)
            coefbc = p.sb("coefbc", [128, 4], F32)
            onesf1 = p.sb("onesf1", [1, 128], F32)
            p.inc(nc.gpsimd.memset(onesf1[:, :], 1.0), ident_s)
            labig = [p.sb(f"labig{j}", [128, KIN * 16], F32) for j in range(3)]
            laTall = p.sb("laTall", [128, KIN * 16], BF16)
            lbbig = p.ln_scr[0:16, 0:OUT]
            lbT = p.sb("lbT", [16, OUT], BF16)
            lbTf = p.ln_xrow[0:16, 0:OUT]
            lo1 = p.sb("lo1", [128, 16], BF16)
            lo1t = p.sb("lo1t", [16, TOK], BF16)
            ploA = p.ps("ploA", [128, 16], F32)
            ploT = p.ps("ploT", [16, 128], BF16)
            lod = p.sem("lod")
            lmm = p.sem("lmm")
            p.dma(nc.sync, lsb[:, :], lsi_in[:, :], lod)
            p.dma(nc.sync, lsf[:, 8:9], bid_in[:, :], lod)
            for j3 in range(3):
                p.dma(nc.sync, labig[j3][:, :].rearrange("r (k c) -> r k c", c=16),
                      lat_in[j3].rearrange("(k r) c -> r k c", r=128), lod)
            lbld = p.sem("lbld")
            p.wait(nc.sync, xtdone)
            p.dma(nc.sync, lbbig[:, :], lbt_in[0], lbld)
            p.wait(nc.vector, lod)
            nc.vector.tensor_scalar(lsf[:, 0:4], lsb[:, :], 1.0, None, ALU.mult)
            nc.vector.drain()
            nc.vector.tensor_scalar(lsf[:, 9:10], lsf[:, 8:9], 0.5, None, ALU.add)
            nc.vector.drain()
            nc.vector.tensor_tensor(lsf[:, 4:8], lsf[:, 0:4],
                                    lsf[:, 9:10].broadcast_to((1, 4)), ALU.is_lt)
            nc.vector.drain()
            nc.vector.reduce_sum(lsf[:, 10:11], lsf[:, 4:8], axis=AX)
            nc.vector.drain()
            nc.vector.tensor_scalar(lsf[:, 11:12], lsf[:, 10:11], 1.5, None, ALU.is_gt)
            nc.vector.tensor_scalar(lsf[:, 12:13], lsf[:, 10:11], -2.0, 0.0, ALU.add, ALU.max)
            nc.vector.drain()
            nc.vector.tensor_scalar(lsf[:, 12:13], lsf[:, 12:13], 2.0, None, ALU.min)
            nc.vector.drain()
            for k3 in range(3):
                nc.vector.tensor_scalar(lsf[:, 13:14], lsf[:, 12:13], -float(k3), None, ALU.add)
                nc.vector.drain()
                nc.vector.tensor_tensor(lsf[:, 13:14], lsf[:, 13:14], lsf[:, 13:14], ALU.mult)
                nc.vector.drain()
                nc.vector.tensor_scalar(lsf[:, 13:14], lsf[:, 13:14], 0.25, None, ALU.is_lt)
                nc.vector.drain()
                nc.vector.tensor_tensor(coef[:, k3:k3 + 1], lsf[:, 13:14],
                                        lsf[:, 11:12], ALU.mult)
                nc.vector.drain()
            p.incd(nc.vector, lod)
            p.wait(nc.tensor, lod)
            p.wait(nc.tensor, ident_s)
            p.inc(nc.tensor.matmul(ploA[:, 0:4], onesf1[:, :], coef[:, 0:4],
                                   start=True, stop=True), lod)
            p.wait(nc.vector, lod)
            p.inc(nc.vector.tensor_copy(coefbc[:, :], ploA[:, 0:4]), lod)
            nc.vector.drain()
            nc.vector.tensor_scalar(laTall[:, :], labig[0][:, :], coefbc[:, 0:1],
                                    None, ALU.mult)
            for j3 in (1, 2):
                nc.vector.scalar_tensor_tensor(laTall[:, :], labig[j3][:, :],
                                               coefbc[:, j3:j3 + 1], laTall[:, :],
                                               ALU.mult, ALU.add)
            p.wait(nc.vector, lbld)
            nc.vector.tensor_scalar(lbTf[:, :], lbbig[:, :], coefbc[0:16, 0:1],
                                    None, ALU.mult)
            p.incd(nc.vector, lod)
            for j3 in (1, 2):
                p.wait(nc.sync, lod)
                p.dma(nc.sync, lbbig[:, :], lbt_in[j3], lbld)
                p.wait(nc.vector, lbld)
                nc.vector.scalar_tensor_tensor(lbTf[:, :], lbbig[:, :],
                                               coefbc[0:16, j3:j3 + 1], lbTf[:, :],
                                               ALU.mult, ALU.add)
                p.incd(nc.vector, lod)
            nc.vector.tensor_copy(lbT[:, :], lbTf[:, :])
            p.incd(nc.vector, lod)
            p.wait(nc.tensor, xtdone)
            p.wait(nc.tensor, lod)
            for t in range(NT):
                for k in range(KIN):
                    mm = nc.tensor.matmul(ploA[:, 0:16], xt[k][:, t * 128:(t + 1) * 128],
                                          laTall[:, k * 16:(k + 1) * 16],
                                          start=(k == 0), stop=(k == KIN - 1))
                p.inc(mm, lmm)
                p.wait(nc.vector, lmm)
                p.inc(nc.vector.tensor_copy(lo1[:, :], ploA[:, 0:16]), lmm)
                p.wait(nc.tensor, lmm)
                p.inc(nc.tensor.transpose(ploT[:, :], lo1[:, :], ident[:, :]), lmm)
                p.wait(nc.vector, lmm)
                p.inc(nc.vector.tensor_copy(lo1t[:, t * 128:(t + 1) * 128], ploT[:, :]),
                      lmm)
            p.wait(nc.tensor, lmm)

        if dequant:
            IH = IN // 2
            qb = p.sb("qb", [128, IH], I32)
            scb = p.sb("scb", [128, IN // 128], F32)
            zrb = p.sb("zrb", [128, IN // 128], F32)
            wdq = p.sb("wdq", [128, IH], BF16)
            dqd = p.sem("dqd")
            dqs = p.sem("dqs")
            for j in range(OUT // 128):
                p.dma(nc.sync, scb[:, :], sc_in[j * 128:(j + 1) * 128, :], dqs)
                p.dma(nc.sync, zrb[:, :], zr_in[j * 128:(j + 1) * 128, :], dqs)
                for hf in range(2):
                    g0 = hf * (IH // 128)
                    p.wait(nc.sync, dqd)
                    p.dma(nc.sync, qb[:, :],
                          qw_in[j * 128:(j + 1) * 128, hf * IH:(hf + 1) * IH], dqs)
                    p.wait(nc.vector, dqs)
                    p.wait(nc.vector, dqd)
                    for g in range(IH // 128):
                        nc.vector.tensor_scalar(wdq[:, g * 128:(g + 1) * 128],
                                                qb[:, g * 128:(g + 1) * 128],
                                                scb[:, g0 + g:g0 + g + 1],
                                                zrb[:, g0 + g:g0 + g + 1],
                                                ALU.mult, ALU.add)
                    p.incd(nc.vector, dqd)
                    p.wait(nc.sync, dqd)
                    p.dma(nc.sync, wdq_d[j * 128:(j + 1) * 128, hf * IH:(hf + 1) * IH],
                          wdq[:, :], dqd)

        p.wait(nc.tensor, xtdone)
        for j in range(NOF):
            b = j % NWB
            if dequant:
                p.wait(nc.sync, mmd, slack=0 if j < NWB else (NWB - 1) * NT)
                p.wait(nc.sync, dqd)
                for k in range(KIN):
                    tsrc = wdq_d[j * OFW:(j + 1) * OFW, k * 128:(k + 1) * 128]
                    p.inc(nc.sync.dma_start_transpose(
                        wbuf[b][:, k * OFW:(k + 1) * OFW], tsrc), wl[b], 16)
            else:
                p.wait(nc.gpsimd, mmd, slack=0 if j < NWB else (NWB - 1) * NT)
                wsrc = w_in[:, j * OFW:(j + 1) * OFW].rearrange("(k r) c -> r k c", r=128)
                p.dma(nc.gpsimd, wbuf[b][:, :].rearrange("r (k c) -> r k c", c=OFW),
                      wsrc, wl[b])
            p.wait(nc.tensor, wl[b])
            p.wait(nc.tensor, bld)
            for t in range(NT):
                bank = t % 4
                u = t % 2
                if u == 0:
                    p.wait(nc.sync, evs)
                    p.dma(nc.sync, resb[:, :].rearrange("r (v c) -> r v c", c=OFW),
                          res_in[t * 128:(t + 2) * 128, j * OFW:(j + 1) * OFW]
                          .rearrange("(v r) c -> r v c", r=128), rld)
                p.wait(nc.tensor, evs, slack=0 if (j == 0 and t < 4) else 3)
                for k in range(KIN):
                    nc.tensor.matmul(psg[bank][:, 0:OFW], xt[k][:, t * 128:(t + 1) * 128],
                                     wbuf[b][:, k * OFW:(k + 1) * OFW],
                                     start=(k == 0), stop=False)
                mm = nc.tensor.matmul(psg[bank][:, 0:OFW], onesb[:, :],
                                      biasb[:, j * OFW:(j + 1) * OFW],
                                      start=False, stop=not lora)
                if lora:
                    mm = nc.tensor.matmul(psg[bank][:, 0:OFW], lo1t[:, t * 128:(t + 1) * 128],
                                          lbT[:, j * OFW:(j + 1) * OFW],
                                          start=False, stop=True)
                p.inc(mm, mmd)
                p.wait(nc.vector, mmd)
                p.wait(nc.vector, rld)
                p.wait(nc.vector, sto)
                p.inc(nc.vector.tensor_tensor(evac[:, u * OFW:(u + 1) * OFW],
                                              psg[bank][:, 0:OFW],
                                              resb[:, u * OFW:(u + 1) * OFW],
                                              ALU.add), evs)
                if u == 1:
                    p.wait(nc.sync, evs)
                    p.dma(nc.sync,
                          out_o[(t - 1) * 128:(t + 1) * 128, j * OFW:(j + 1) * OFW]
                          .rearrange("(v r) c -> r v c", r=128),
                          evac[:, :].rearrange("r (v c) -> r v c", c=OFW), sto)
        p.wait(nc.sync, sto)


def mm_io(nc, pre, IN, OUT, ln, dequant, lora, scale_in):
    io = {"x": nc.dram_tensor(pre + "x", [TOK, IN], F32 if ln else BF16, kind="ExternalInput"),
          "res": nc.dram_tensor(pre + "res", [TOK, OUT], F32, kind="ExternalInput"),
          "bias": nc.dram_tensor(pre + "bias", [1, OUT], F32, kind="ExternalInput"),
          "out": nc.dram_tensor(pre + "out", [TOK, OUT], F32, kind="ExternalOutput")}
    if dequant:
        io["qw"] = nc.dram_tensor(pre + "qw", [OUT, IN], I32, kind="ExternalInput")
        io["sc"] = nc.dram_tensor(pre + "sc", [OUT, IN // 128], F32, kind="ExternalInput")
        io["zr"] = nc.dram_tensor(pre + "zr", [OUT, IN // 128], F32, kind="ExternalInput")
        io["wdq"] = nc.dram_tensor(pre + "wdq_scratch", [OUT, IN], BF16, kind="Internal")
    else:
        io["w"] = nc.dram_tensor(pre + "w", [IN, OUT], F32, kind="ExternalInput")
    if scale_in:
        io["nw"] = nc.dram_tensor(pre + "nw", [128, IN // 128], F32, kind="ExternalInput")
    if lora:
        io["lsi"] = nc.dram_tensor(pre + "lsi", [1, 4], I32, kind="ExternalInput")
        io["bid"] = nc.dram_tensor(pre + "bid", [1, 1], F32, kind="ExternalInput")
        io["lat"] = nc.dram_tensor(pre + "lat", [3, IN, 16], F32, kind="ExternalInput")
        io["lbt"] = nc.dram_tensor(pre + "lbt", [3, 16, OUT], F32, kind="ExternalInput")
    return io


def build_mm(name, IN, OUT, ln, dequant, lora, scale_in, OFW=512):
    p = P(name)
    io = mm_io(p.nc, "", IN, OUT, ln, dequant, lora, scale_in)
    emit_mm(p, IN, OUT, ln, dequant, lora, scale_in, io, OFW)
    return p.nc


def _c(a):
    return np.ascontiguousarray(a)


def kernel(**inputs):
    return kernel_fused(**inputs)


def kernel_unfused(**inputs):
    f32 = np.float32
    hid = np.asarray(inputs["L0_attn_hidden_states"], f32).reshape(8192, D)
    wT = _c(np.asarray(inputs["L0_attn_in_proj_w"], f32).T)
    conv_w = np.asarray(inputs["L0_attn_conv_w"], f32).reshape(CONV_DIM, 4)
    conv_bT = _c(np.asarray(inputs["L0_attn_conv_b"], f32).reshape(80, 128).T)
    dt_bias = _c(np.asarray(inputs["L0_attn_dt_bias"], f32).reshape(128, 1))
    A_log = _c(np.asarray(inputs["L0_attn_A_log"], f32).reshape(128, 1))
    Dv = _c(np.asarray(inputs["L0_attn_D"], f32).reshape(1, H))
    nwT = _c(np.asarray(inputs["L0_attn_norm_weight"], f32).reshape(INTER // 128, 128).T)
    opT = _c(np.asarray(inputs["L0_attn_out_proj_w"], f32).T)
    qw = _c(np.asarray(inputs["L0_ffn_qweight"]).reshape(D, D).astype(np.int32))
    sc = _c(np.asarray(inputs["L0_ffn_scales"], f32).reshape(D, 32))
    zr = _c(np.asarray(inputs["L0_ffn_scaled_zeros"], f32).reshape(D, 32))
    fb = _c(np.asarray(inputs["L0_ffn_bias"], f32).reshape(1, D))
    wqk = np.asarray(inputs["L1_attn_Wqk_w"], f32)
    wqT = _c(wqk[0::2].T)
    wkT = _c(wqk[1::2].T)
    bqk = np.asarray(inputs["L1_attn_Wqk_b"], f32)
    qbT = _c(bqk[0::2].reshape(4, 128).T)
    kbT = _c(bqk[1::2].reshape(4, 128).T)
    wvT = _c(np.asarray(inputs["L1_attn_Wv_w"], f32).T)
    vb = _c(np.asarray(inputs["L1_attn_Wv_b"], f32).reshape(1, 512))
    woT = _c(np.asarray(inputs["L1_attn_Wo_w"], f32).T)
    wob = _c(np.asarray(inputs["L1_attn_Wo_b"], f32).reshape(1, D))
    w1T = _c(np.asarray(inputs["L1_ffn_weight"], f32).T)
    db = _c(np.asarray(inputs["L1_ffn_bias"], f32).reshape(1, D))
    latT = _c(np.asarray(inputs["L1_ffn_lora_A"], f32).transpose(0, 2, 1))
    lbtT = _c(np.asarray(inputs["L1_ffn_lora_B"], f32).transpose(0, 2, 1))
    lsi = _c(np.asarray(inputs["L1_ffn_lora_start_indices"]).reshape(1, 4).astype(np.int32))
    zeros_bias = np.zeros((1, D), f32)
    cores = list(range(NCORES))

    def run(nc, maps):
        return run_bass_kernel_spmd(nc, maps, core_ids=cores).results

    sl = lambda a, c: _c(a[c * TOK:(c + 1) * TOK])

    r1 = run(build_A1(), [{"x": sl(hid, c), "w": wT} for c in cores])
    gate = [np.asarray(r["gate"]) for r in r1]
    hbc = [np.asarray(r["hbc"]) for r in r1]
    dtc = [np.asarray(r["dt"]) for r in r1]

    import ml_dtypes
    BFnp = ml_dtypes.bfloat16
    maps = []
    for c in cores:
        halo = np.zeros((CONV_DIM, 3), BFnp) if c % 4 == 0 else hbc[c - 1][:, -3:]
        maps.append(dict(hbc=_c(np.concatenate([halo, hbc[c]], axis=1)), dt=_c(dtc[c]),
                         cw=conv_w, cb=conv_bT, dtb=dt_bias, al=A_log))
    r2 = run(build_A2(), maps)

    eall = _c(np.concatenate([np.asarray(r["eout"]) for r in r2], axis=0))
    logdall = _c(np.concatenate([np.asarray(r["logd"]) for r in r2], axis=0))
    e8sel = np.zeros((NCORES, NCORES * 128), f32)
    for q in range(NCORES):
        e8sel[q, q * 128:(q + 1) * 128] = 1.0
    maps = []
    for c in cores:
        g0 = 4 * (c // 4)
        grp = range(g0, g0 + 4)
        m8 = np.zeros((NCORES, NCORES), f32)
        mv = np.zeros((NCORES, 1), f32)
        for q in grp:
            if q < c:
                mv[q, 0] = 1.0
                for r in grp:
                    if q < r < c:
                        m8[r, q] = 1.0
        maps.append(dict(ydiag=np.asarray(r2[c]["ydiag"]), hs=np.asarray(r2[c]["hs"]),
                         gate=gate[c], ccm=np.asarray(r2[c]["ccm"]),
                         pfx=np.asarray(r2[c]["pfx"]), acs=np.asarray(r2[c]["acs"]),
                         dpre=np.asarray(r2[c]["dpre"]), eall=eall, logdall=logdall,
                         mask8t=m8, maskv=mv, e8sel=e8sel, dvec=Dv))
    r3a = run(build_A3a(), maps)

    maps = [dict(x=np.asarray(r3a[c]["xn"]), res=sl(hid, c), bias=zeros_bias,
                 w=opT, nw=nwT) for c in cores]
    r3b = run(build_mm("a3b", INTER, D, False, False, False, True, OFW=256), maps)
    x1 = [np.asarray(r["out"]) for r in r3b]

    maps = [dict(x=x1[c], res=x1[c], bias=fb, qw=qw, sc=sc, zr=zr) for c in cores]
    rb = run(build_mm("bffn", D, D, True, True, False, False), maps)
    x2 = [np.asarray(r["out"]) for r in rb]

    maps = [dict(x=x2[c], wq=wqT, wk=wkT, wv=wvT, qb=qbT, kb=kbT, vb=vb) for c in cores]
    rc1 = run(build_C1(), maps)
    kts = [np.asarray(r["kt"]) for r in rc1]
    vs = [np.asarray(r["v"]) for r in rc1]

    maps = []
    for c in cores:
        g0 = 4 * (c // 4)
        ktf = _c(np.concatenate([kts[g0 + i] for i in range(4)], axis=1))
        vf = _c(np.concatenate([vs[g0 + i] for i in range(4)], axis=0))
        maps.append(dict(qt=np.asarray(rc1[c]["qt"]), ktf=ktf, vf=vf, wo=woT,
                         wob=wob, res=x2[c]))
    rc2 = run(build_C2(), maps)
    x3 = [np.asarray(r["out"]) for r in rc2]

    maps = [dict(x=x3[c], res=x3[c], bias=db, w=w1T, lsi=lsi,
                 bid=np.full((1, 1), float(c // 4), f32), lat=latT, lbt=lbtT)
            for c in cores]
    rd = run(build_mm("dffn", D, D, True, False, True, False), maps)
    out = np.concatenate([np.asarray(r["out"]) for r in rd], axis=0)
    return out.reshape(2, 4096, D)


def emit_A3a(p, yd_in, hs_in, gt_in, ccm_in, pfx_in, acs_in, dpre_in, eall_in,
             logd_in, m8_in, mv_in, sel_in, d_in, xn_o, NW=NCORES, dbg=False):
    nc = p.nc
    if dbg:
        dbg_pin = nc.dram_tensor("dbg_pin", [128, INTER], F32, kind="ExternalOutput")
        dbg_y = nc.dram_tensor("dbg_y", [TOK, INTER], F32, kind="ExternalOutput")
        dbg_pvf = nc.dram_tensor("dbg_pvf", [TOK, INTER], F32, kind="ExternalOutput")
        dbg_xg = nc.dram_tensor("dbg_xg", [TOK, INTER], F32, kind="ExternalOutput")
        dbg_gs = nc.dram_tensor("dbg_gs", [TOK, INTER], F32, kind="ExternalOutput")
    with p.stack:
        onesf1 = p.sb("onesf1", [1, 128], F32)
        cst = p.sem("cst0")
        p.inc(nc.gpsimd.memset(onesf1[:, :], 1.0), cst)
        b_eps, s_half = p.ccol(EPS), p.ccol(-0.5)

        pMisc = p.ps("pMisc", [128, 512], F32)
        pY = [p.ps(f"pY{i}", [128, 512], F32) for i in range(2)]

        pin = p.sb("pin", [128, INTER], F32)
        ebuf = p.sb("ebuf", [128, INTER], F32)
        hsb = p.sb("hsb", [128, INTER], BF16)
        gtb = p.sb("gtb", [128, INTER], BF16)
        pfb = p.sb("pfb", [128, INTER], BF16)
        pvf = p.sb("pvf", [128, INTER], BF16)
        sqs = p.sb("sqs", [128, INTER], F32)
        ccmb = [p.sb(f"ccmb{g}", [128, TOK], BF16) for g in range(G)]
        dpreb = p.sb("dpreb", [128, NCHUNK * H], F32)
        coefs = p.sb("coefs", [NW, H], F32)
        coefbc = p.sb("coefbc", [128, NW * 128], F32)
        logdb = p.sb("logdb", [NW, H], F32)
        m8b = p.sb("m8b", [NW, NW], F32)
        mvb = p.sb("mvb", [NW, 1], F32)
        selb = p.sb("selb", [NW, NW * 128], F32)
        dvb = p.sb("dvb", [1, H], F32)
        dbc = p.sb("dbc", [128, H], F32)
        acsb = p.sb("acsb", [128, H], F32)
        sdob = p.sb("sdob", [128, H], F32)
        msb = p.sb("msb", [128, 2 * G], F32)

        ld = p.sem("ld"); eld = p.sem("eld"); cfd = p.sem("cfd")
        pind = p.sem("pind"); chld = p.sem("chld"); pvb = p.sem("pvb")
        ymd = p.sem("ymd"); yev = p.sem("yev"); hbd = p.sem("hbd")
        pld = p.sem("pld"); gnd = p.sem("gnd"); sto = p.sem("sto")

        p.dma(nc.sync, logdb[:, :], logd_in[:, :], ld)
        p.dma(nc.sync, m8b[:, :], m8_in[:, :], ld)
        p.dma(nc.sync, mvb[:, :], mv_in[:, :], ld)
        p.dma(nc.sync, selb[:, :], sel_in[:, :], ld)
        p.dma(nc.sync, dvb[:, :], d_in[:, :], ld)
        p.dma(nc.sync, dpreb[:, :], dpre_in[:, :], ld)
        for g in range(G):
            p.dma(nc.gpsimd, ccmb[g][:, :], ccm_in[g * 128:(g + 1) * 128, :], chld)

        p.wait(nc.tensor, ld)
        p.inc(nc.tensor.matmul(pMisc[0:NW, 0:H], m8b[:, :], logdb[:, :],
                               start=True, stop=True), cfd)
        p.wait(nc.scalar, cfd)
        nc.scalar.activation(coefs[:, :], pMisc[0:NW, 0:H], AF.Exp)
        p.incd(nc.scalar, cfd)
        p.wait(nc.vector, cfd)
        nc.vector.tensor_scalar(coefs[:, :], coefs[:, :], mvb[:, 0:1], None, ALU.mult)
        p.incd(nc.vector, cfd)
        p.wait(nc.tensor, cfd)
        for q in range(NW):
            p.inc(nc.tensor.matmul(pMisc[:, 128:256], selb[:, q * 128:(q + 1) * 128],
                                   coefs[:, :], start=True, stop=True), cfd)
            p.wait(nc.vector, cfd)
            p.inc(nc.vector.tensor_copy(coefbc[:, q * 128:(q + 1) * 128],
                                        pMisc[:, 128:256]), cfd)
            p.wait(nc.tensor, cfd)
        p.wait(nc.tensor, cst)
        p.inc(nc.tensor.matmul(pMisc[:, 256:384], onesf1[:, :], dvb[:, :],
                               start=True, stop=True), cfd)
        p.wait(nc.vector, cfd)
        p.inc(nc.vector.tensor_copy(dbc[:, :], pMisc[:, 256:384]), cfd)

        nc.vector.memset(pin[:, :], 0.0)
        for q in range(NW):
            p.wait(nc.sync, pind)
            p.dma(nc.sync, ebuf[:, :], eall_in[q * 128:(q + 1) * 128, :], eld)
            p.wait(nc.vector, eld)
            cap = coefbc[:, q * 128:(q + 1) * 128].unsqueeze(2).broadcast_to((128, H, Pd))
            nc.vector.tensor_tensor(sqs[:, :].rearrange("n (h q2) -> n h q2", q2=Pd),
                                    ebuf[:, :].rearrange("n (h q2) -> n h q2", q2=Pd),
                                    cap, ALU.mult)
            p.inc(nc.vector.tensor_add(pin[:, :], pin[:, :], sqs[:, :]), pind)

        for c in range(NCHUNK):
            cc = c * 128
            p.wait(nc.sync, pind)
            p.wait(nc.sync, gnd)
            p.wait(nc.gpsimd, gnd)
            p.dma(nc.sync, ebuf[:, :], yd_in[cc:cc + 128, :], chld)
            p.dma(nc.gpsimd, hsb[:, :], hs_in[cc:cc + 128, :], chld)
            p.dma(nc.gpsimd, gtb[:, :], gt_in[cc:cc + 128, :], chld)
            p.dma(nc.gpsimd, pfb[:, :], pfx_in[cc:cc + 128, :], chld)
            p.dma(nc.sync, acsb[:, :], acs_in[cc:cc + 128, :], chld)
            p.wait(nc.scalar, chld)
            p.inc(nc.scalar.activation(sdob[:, :], acsb[:, :], AF.Exp), chld)
            p.wait(nc.vector, chld)
            p.wait(nc.vector, ymd)
            p.wait(nc.vector, sto)
            dap = dpreb[:, c * H:(c + 1) * H].unsqueeze(2).broadcast_to((128, H, Pd))
            nc.vector.tensor_tensor(sqs[:, :].rearrange("n (h q2) -> n h q2", q2=Pd),
                                    pin[:, :].rearrange("n (h q2) -> n h q2", q2=Pd),
                                    dap, ALU.mult)
            p.inc(nc.vector.tensor_add(pvf[:, :], sqs[:, :], pfb[:, :]), pvb)
            if dbg:
                p.wait(nc.sync, pvb)
                p.dma(nc.gpsimd, dbg_pvf[cc:cc + 128, :], pvf[:, :], sto)
                if c == 0:
                    p.wait(nc.sync, pind)
                    p.dma(nc.sync, dbg_pin[:, :], pin[:, :], sto)
            p.wait(nc.tensor, pvb)
            p.wait(nc.tensor, chld)
            for hb in range(32):
                h0 = hb * 4
                b2 = hb % 2
                p.wait(nc.tensor, yev, slack=0 if (c == 0 and hb < 2) else 1)
                for i in range(4):
                    mm = nc.tensor.matmul(pY[b2][:, i * 64:(i + 1) * 64],
                                          ccmb[(h0 + i) % G][:, cc:cc + 128],
                                          pvf[:, (h0 + i) * Pd:(h0 + i + 1) * Pd],
                                          start=True, stop=True)
                    p.inc(mm, ymd)
                sl = slice(h0 * Pd, (h0 + 4) * Pd)
                p.wait(nc.vector, ymd)
                sap = sdob[:, h0:h0 + 4].unsqueeze(2).broadcast_to((128, 4, Pd))
                nc.vector.tensor_tensor(
                    sqs[:, sl].rearrange("l (i q2) -> l i q2", q2=Pd),
                    pY[b2][:, 0:256].rearrange("l (i q2) -> l i q2", q2=Pd),
                    sap, ALU.mult)
                p.inc(nc.vector.tensor_add(ebuf[:, sl], ebuf[:, sl], sqs[:, sl]), yev)
                dapp = dbc[:, h0:h0 + 4].unsqueeze(2).broadcast_to((128, 4, Pd))
                p.wait(nc.gpsimd, yev)
                nc.gpsimd.tensor_tensor(
                    sqs[:, sl].rearrange("l (i q2) -> l i q2", q2=Pd),
                    hsb[:, sl].rearrange("l (i q2) -> l i q2", q2=Pd),
                    dapp, ALU.mult)
                p.inc(nc.gpsimd.tensor_add(ebuf[:, sl], ebuf[:, sl], sqs[:, sl]), hbd)
            if dbg:
                p.wait(nc.sync, hbd)
                p.dma(nc.sync, dbg_y[cc:cc + 128, :], ebuf[:, :], sto)
            p.wait(nc.scalar, hbd)
            if dbg:
                p.wait(nc.scalar, sto)
            p.inc(nc.scalar.activation(gtb[:, :], gtb[:, :], AF.Silu), hbd)
            p.wait(nc.vector, hbd)
            nc.vector.tensor_mul(ebuf[:, :], ebuf[:, :], gtb[:, :])
            if dbg:
                p.incd(nc.vector, hbd)
                p.wait(nc.sync, hbd)
                p.dma(nc.sync, dbg_xg[cc:cc + 128, :], ebuf[:, :], sto)
                p.dma(nc.gpsimd, dbg_gs[cc:cc + 128, :], gtb[:, :], sto)
                p.wait(nc.vector, sto)
            nc.vector.tensor_tensor(sqs[:, :], ebuf[:, :], ebuf[:, :], ALU.mult)
            for g in range(G):
                nc.vector.reduce_sum(msb[:, g:g + 1], sqs[:, g * 1024:(g + 1) * 1024],
                                     axis=AX)
            p.incd(nc.vector, gnd)
            nc.vector.drain()
            nc.vector.tensor_scalar(msb[:, G:2 * G], msb[:, 0:G], 1.0 / 1024.0,
                                    None, ALU.mult)
            p.incd(nc.vector, gnd)
            p.wait(nc.scalar, gnd)
            p.wait_consts(nc.scalar)
            nc.scalar.activation(msb[:, 0:G], msb[:, G:2 * G], AF.Ln, bias=b_eps)
            nc.scalar.drain()
            nc.scalar.activation(msb[:, G:2 * G], msb[:, 0:G], AF.Exp, scale=s_half)
            p.incd(nc.scalar, gnd)
            p.wait(nc.vector, gnd)
            for g in range(G):
                nc.vector.tensor_scalar(pvf[:, g * 1024:(g + 1) * 1024],
                                        ebuf[:, g * 1024:(g + 1) * 1024],
                                        msb[:, G + g:G + g + 1], None, ALU.mult)
            p.incd(nc.vector, gnd)
            p.wait(nc.sync, gnd)
            p.dma(nc.sync, xn_o[cc:cc + 128, :], pvf[:, :], sto)
        p.wait(nc.sync, sto)


def a3a_io(nc, pre, NW=NCORES):
    return [nc.dram_tensor(pre + "ydiag", [TOK, INTER], F32, kind="ExternalInput"),
            nc.dram_tensor(pre + "hs", [TOK, INTER], BF16, kind="ExternalInput"),
            nc.dram_tensor(pre + "gate", [TOK, INTER], BF16, kind="ExternalInput"),
            nc.dram_tensor(pre + "ccm", [G * Nd, TOK], BF16, kind="ExternalInput"),
            nc.dram_tensor(pre + "pfx", [TOK, INTER], BF16, kind="ExternalInput"),
            nc.dram_tensor(pre + "acs", [TOK, H], F32, kind="ExternalInput"),
            nc.dram_tensor(pre + "dpre", [128, NCHUNK * H], F32, kind="ExternalInput"),
            nc.dram_tensor(pre + "eall", [NW * 128, INTER], F32, kind="ExternalInput"),
            nc.dram_tensor(pre + "logdall", [NW, H], F32, kind="ExternalInput"),
            nc.dram_tensor(pre + "mask8t", [NW, NW], F32, kind="ExternalInput"),
            nc.dram_tensor(pre + "maskv", [NW, 1], F32, kind="ExternalInput"),
            nc.dram_tensor(pre + "e8sel", [NW, NW * 128], F32, kind="ExternalInput"),
            nc.dram_tensor(pre + "dvec", [1, H], F32, kind="ExternalInput"),
            nc.dram_tensor(pre + "xn", [TOK, INTER], BF16, kind="ExternalOutput")]


def build_A3a(dbg=False):
    p = P("a3a")
    io = a3a_io(p.nc, "")
    emit_A3a(p, *io, NW=NCORES, dbg=dbg)
    return p.nc


def emit_C1(p, x_in, wq_in, wk_in, wv_in, qb_in, kb_in, vb_in, qt_o, kt_o, v_o):
    nc = p.nc
    HEAD = 512
    with p.stack:
        ident, ident_s = p.make_identity(BF16)
        p.pst_ln = [p.ps("pst_ln0", [128, 128], BF16), p.ps("pst_ln1", [128, 128], BF16)]
        xt = [p.sb(f"xt{k}", [128, TOK], BF16) for k in range(KD)]
        onesb = p.sb("onesb", [1, 128], BF16)
        p.inc(nc.gpsimd.memset(onesb[:, :], 1.0), ident_s)
        wbuf = p.sb("wbuf", [128, KD * HEAD // 8], BF16)
        wbig = p.sb("wbig", [128, KD * 512], BF16)
        qbb = p.sb("qbb", [128, HEAD // 128], F32)
        kbb = p.sb("kbb", [128, HEAD // 128], F32)
        vbb = p.sb("vbb", [1, HEAD], BF16)
        ev = p.sb("ev", [128, 512], BF16)
        psg = [p.ps(f"psg{i}", [128, 512], F32) for i in range(2)]
        wl = p.sem("wl"); bl = p.sem("bl"); mmd = p.sem("mmd")
        evs = p.sem("evs"); sto = p.sem("sto")
        p.dma(nc.sync, qbb[:, :], qb_in[:, :], bl)
        p.dma(nc.sync, kbb[:, :], kb_in[:, :], bl)
        p.dma(nc.gpsimd, vbb[:, :], vb_in[:, :], bl)
        xtdone = ln_and_transpose(p, x_in, xt, ident, ident_s)
        p.wait(nc.tensor, xtdone)
        p.wait(nc.tensor, bl)
        for wi, (w_in_t, bcol, out_t) in enumerate([(wq_in, qbb, qt_o), (wk_in, kbb, kt_o)]):
            p.wait(nc.gpsimd, mmd)
            src = w_in_t[:, :].rearrange("(k r) c -> r k c", r=128)
            p.dma(nc.gpsimd, wbig[:, :].rearrange("r (k c) -> r k c", c=512), src, wl)
            p.wait(nc.tensor, wl)
            for f in range(HEAD // 128):
                for half in range(2):
                    bank = half % 2
                    p.wait(nc.tensor, evs, slack=0 if (wi == 0 and f == 0) else 1)
                    for k in range(KD):
                        mm = nc.tensor.matmul(psg[bank][:, :],
                                              wbig[:, k * 512 + f * 128:k * 512 + (f + 1) * 128],
                                              xt[k][:, half * 512:(half + 1) * 512],
                                              start=(k == 0), stop=(k == KD - 1))
                    p.inc(mm, mmd)
                    p.wait(nc.vector, mmd)
                    p.wait(nc.vector, sto)
                    p.inc(nc.vector.tensor_scalar(ev[:, :], psg[bank][:, :],
                                                  bcol[:, f:f + 1], None, ALU.add), evs)
                    p.wait(nc.sync, evs)
                    p.dma(nc.sync, out_t[f * 128:(f + 1) * 128,
                                         half * 512:(half + 1) * 512], ev[:, :], sto)
        p.wait(nc.gpsimd, mmd)
        src = wv_in[:, :].rearrange("(k r) c -> r k c", r=128)
        p.dma(nc.gpsimd, wbig[:, :].rearrange("r (k c) -> r k c", c=512), src, wl)
        p.wait(nc.tensor, wl)
        for t in range(NT):
            bank = t % 2
            p.wait(nc.tensor, evs, slack=0 if t < 2 else 1)
            for k in range(KD):
                nc.tensor.matmul(psg[bank][:, :], xt[k][:, t * 128:(t + 1) * 128],
                                 wbig[:, k * 512:(k + 1) * 512],
                                 start=(k == 0), stop=False)
            mm = nc.tensor.matmul(psg[bank][:, :], onesb[:, :], vbb[:, :],
                                  start=False, stop=True)
            p.inc(mm, mmd)
            p.wait(nc.vector, mmd)
            p.wait(nc.vector, sto)
            p.inc(nc.vector.tensor_copy(ev[:, :], psg[bank][:, :]), evs)
            p.wait(nc.sync, evs)
            p.dma(nc.sync, v_o[t * 128:(t + 1) * 128, :], ev[:, :], sto)
        p.wait(nc.sync, sto)


def c1_io(nc, pre):
    HEAD = 512
    return [nc.dram_tensor(pre + "x", [TOK, D], F32, kind="ExternalInput"),
            nc.dram_tensor(pre + "wq", [D, HEAD], F32, kind="ExternalInput"),
            nc.dram_tensor(pre + "wk", [D, HEAD], F32, kind="ExternalInput"),
            nc.dram_tensor(pre + "wv", [D, HEAD], F32, kind="ExternalInput"),
            nc.dram_tensor(pre + "qb", [128, HEAD // 128], F32, kind="ExternalInput"),
            nc.dram_tensor(pre + "kb", [128, HEAD // 128], F32, kind="ExternalInput"),
            nc.dram_tensor(pre + "vb", [1, HEAD], F32, kind="ExternalInput"),
            nc.dram_tensor(pre + "qt", [HEAD, TOK], BF16, kind="ExternalOutput"),
            nc.dram_tensor(pre + "kt", [HEAD, TOK], BF16, kind="ExternalOutput"),
            nc.dram_tensor(pre + "v", [TOK, HEAD], BF16, kind="ExternalOutput")]


def build_C1():
    p = P("c1")
    emit_C1(p, *c1_io(p.nc, ""))
    return p.nc


def emit_C2(p, qt_in, kt_in, v_in, wo_in, wob_in, res_in, out_o,
            kt_ranked=False, nh=8, ntt=NT, do_proj=True):
    nc = p.nc
    HEAD = 512
    E = 64
    SEQ = 4096
    ISCL = 1.0 / float(np.sqrt(512.0))
    ksc, ktc, it2 = 0, 0, 0
    with p.stack:
        ident, ident_s = p.make_identity(BF16)
        onesb = p.sb("onesb", [1, 128], BF16)
        p.inc(nc.gpsimd.memset(onesb[:, :], 1.0), ident_s)
        qall = p.sb("qall", [64, 8 * TOK], BF16)
        kh = [p.sb(f"kh{i}", [64, SEQ], BF16) for i in range(2)]
        vh = [p.sb(f"vh{i}", [128, (SEQ // 128) * E], BF16) for i in range(2)]
        ssb = p.sb("ssb", [128, SEQ], F32)
        asb = p.sb("asb", [128, SEQ], BF16)
        aTb = p.sb("aTb", [128, SEQ], BF16)
        xo = p.sb("xo", [128, NT * HEAD], BF16)
        wosb = p.sb("wosb", [128, 4 * D], BF16)
        wobb = p.sb("wobb", [1, D], BF16)
        xoT = p.sb("xoT", [128, 4 * 128], BF16)
        resb = p.sb("resb", [128, D], F32)
        evac = p.sb("evac", [128, D], F32)
        smx = p.sb("smx", [128, 4], F32)
        pS = [p.ps(f"pS{i}", [128, 512], F32) for i in range(2)]
        pTb = [p.ps("pT0", [128, 128], BF16), p.ps("pT1", [128, 128], BF16)]
        pT = pTb[0]
        pO = [p.ps(f"pO{i}", [128, 64], F32) for i in range(2)]
        pP = [p.ps(f"pP{i}", [128, 512], F32) for i in range(2)]

        qld = p.sem("qld"); khl = p.sem("khl"); vhl = p.sem("vhl")
        smm = p.sem("smm"); sev = p.sem("sev"); sfx = p.sem("sfx")
        trd = p.sem("trd"); tev = p.sem("tev"); omm = p.sem("omm")
        oev = p.sem("oev"); wld = p.sem("wld"); pmm = p.sem("pmm")
        pev = p.sem("pev"); rld = p.sem("rld"); sto = p.sem("sto")
        xod = p.sem("xod")

        for h in range(8):
            p.dma(nc.sync, qall[:, h * TOK:(h + 1) * TOK],
                  qt_in[h * 64:(h + 1) * 64, :], qld)
        src = wo_in[:, :].rearrange("(k r) c -> r k c", r=128)
        p.dma(nc.gpsimd, wosb[:, :].rearrange("r (k c) -> r k c", c=D), src, wld)
        p.dma(nc.gpsimd, wobb[:, :], wob_in[:, :], wld)
        for h in range(nh):
            bh = h % 2
            p.wait(nc.sync, smm)
            if kt_ranked:
                for rk in range(4):
                    p.dma(nc.sync, kh[bh][:, rk * TOK:(rk + 1) * TOK],
                          kt_in[rk * HEAD + h * 64:rk * HEAD + (h + 1) * 64, :], khl)
            else:
                p.dma(nc.sync, kh[bh][:, :], kt_in[h * 64:(h + 1) * 64, :], khl)
            p.wait(nc.sync, omm)
            p.dma(nc.sync, vh[bh][:, :].rearrange("r (k c) -> r k c", c=E),
                  v_in[:, h * 64:(h + 1) * 64].rearrange("(k r) c -> r k c", r=128), vhl)
            p.wait(nc.tensor, khl)
            p.wait(nc.tensor, qld)
            for t in range(ntt):
                for kc in range(SEQ // 512):
                    bank = ksc % 2
                    p.wait(nc.tensor, sev, slack=0 if ksc < 2 else 1)
                    ksc += 1
                    p.inc(nc.tensor.matmul(pS[bank][:, :],
                                           qall[:, h * TOK + t * 128:h * TOK + (t + 1) * 128],
                                           kh[bh][:, kc * 512:(kc + 1) * 512],
                                           start=True, stop=True), smm)
                    p.wait(nc.vector, smm)
                    p.inc(nc.vector.tensor_copy(ssb[:, kc * 512:(kc + 1) * 512],
                                                pS[bank][:, :]), sev)
                nc.vector.reduce_max(smx[:, 0:1], ssb[:, :], axis=AX)
                p.incd(nc.vector, sfx)
                nc.vector.tensor_scalar(ssb[:, :], ssb[:, :], smx[:, 0:1], None, ALU.subtract)
                p.incd(nc.vector, sfx)
                p.wait(nc.scalar, sfx)
                p.inc(nc.scalar.activation(ssb[:, :], ssb[:, :], AF.Exp), sfx)
                p.wait(nc.vector, sfx)
                nc.vector.reduce_sum(smx[:, 1:2], ssb[:, :], axis=AX)
                nc.vector.drain()
                nc.vector.reciprocal(smx[:, 2:3], smx[:, 1:2])
                nc.vector.drain()

                p.inc(nc.vector.tensor_scalar(asb[:, :], ssb[:, :], smx[:, 2:3],
                                              ISCL, ALU.mult, ALU.mult), sfx)
                p.wait(nc.tensor, sfx)
                p.wait(nc.tensor, vhl)
                for kc in range(SEQ // 128):
                    bk = ktc % 2
                    p.wait(nc.tensor, tev, slack=0 if ktc < 2 else 1)
                    ktc += 1
                    p.inc(nc.tensor.transpose(pTb[bk][:, :],
                                              asb[:, kc * 128:(kc + 1) * 128],
                                              ident[:, :]), trd)
                    p.wait(nc.vector, trd)
                    p.inc(nc.vector.tensor_copy(aTb[:, kc * 128:(kc + 1) * 128],
                                                pTb[bk][:, :]), tev)
                bo = it2 % 2
                p.wait(nc.tensor, tev)
                p.wait(nc.tensor, oev, slack=0 if it2 < 2 else 1)
                it2 += 1
                for kc in range(SEQ // 128):
                    mm = nc.tensor.matmul(pO[bo][:, :], aTb[:, kc * 128:(kc + 1) * 128],
                                          vh[bh][:, kc * E:(kc + 1) * E],
                                          start=(kc == 0), stop=(kc == SEQ // 128 - 1))
                p.inc(mm, omm)
                p.wait(nc.vector, omm)
                p.wait(nc.vector, xod)
                p.inc(nc.vector.tensor_copy(xo[:, t * HEAD + h * E:t * HEAD + (h + 1) * E],
                                            pO[bo][:, :]), oev)
        if not do_proj:
            p.wait(nc.sync, oev)
            p.wait(nc.sync, wld)
            return
        p.wait(nc.tensor, oev)
        p.wait(nc.tensor, wld)
        for t in range(NT):
            p.wait(nc.sync, pev)
            p.dma(nc.sync, resb[:, :], res_in[t * 128:(t + 1) * 128, :], rld)
            for k in range(4):
                p.wait(nc.tensor, xod, slack=0 if (t == 0 and k < 2) else 1)
                p.inc(nc.tensor.transpose(pTb[k % 2][:, :],
                                          xo[:, t * HEAD + k * 128:t * HEAD + (k + 1) * 128],
                                          ident[:, :]), trd)
                p.wait(nc.vector, trd)
                p.inc(nc.vector.tensor_copy(xoT[:, k * 128:(k + 1) * 128],
                                            pTb[k % 2][:, :]), xod)
            p.wait(nc.tensor, xod)
            for j in range(D // 512):
                bank = j % 2
                p.wait(nc.tensor, pev, slack=0 if (t * 8 + j) < 2 else 1)
                for k in range(4):
                    nc.tensor.matmul(pP[bank][:, :], xoT[:, k * 128:(k + 1) * 128],
                                     wosb[:, k * D + j * 512:k * D + (j + 1) * 512],
                                     start=(k == 0), stop=False)
                mm = nc.tensor.matmul(pP[bank][:, :], onesb[:, :],
                                      wobb[:, j * 512:(j + 1) * 512],
                                      start=False, stop=True)
                p.inc(mm, pmm)
                p.wait(nc.vector, pmm)
                p.wait(nc.vector, rld)
                p.wait(nc.vector, sto)
                p.inc(nc.vector.tensor_add(evac[:, j * 512:(j + 1) * 512],
                                           pP[bank][:, :],
                                           resb[:, j * 512:(j + 1) * 512]), pev)
            p.wait(nc.sync, pev)
            p.dma(nc.sync, out_o[t * 128:(t + 1) * 128, :], evac[:, :], sto)
        p.wait(nc.sync, sto)


def build_C2(nh=8, ntt=NT, do_proj=True):
    p = P("c2")
    nc = p.nc
    HEAD, SEQ = 512, 4096
    io = [nc.dram_tensor("qt", [HEAD, TOK], BF16, kind="ExternalInput"),
          nc.dram_tensor("ktf", [HEAD, SEQ], BF16, kind="ExternalInput"),
          nc.dram_tensor("vf", [SEQ, HEAD], BF16, kind="ExternalInput"),
          nc.dram_tensor("wo", [HEAD, D], F32, kind="ExternalInput"),
          nc.dram_tensor("wob", [1, D], F32, kind="ExternalInput"),
          nc.dram_tensor("res", [TOK, D], F32, kind="ExternalInput"),
          nc.dram_tensor("out", [TOK, D], F32, kind="ExternalOutput")]
    emit_C2(p, *io, kt_ranked=False, nh=nh, ntt=ntt, do_proj=do_proj)
    return p.nc


def build_fused():
    p = P("mk")
    nc = p.nc
    HEAD = 512
    x_in = nc.dram_tensor("x", [TOK, D], F32, kind="ExternalInput")
    xh_in = nc.dram_tensor("xh", [3, D], F32, kind="ExternalInput")
    w_ip = nc.dram_tensor("w_ip", [D, PROJ], F32, kind="ExternalInput")
    cw_in = nc.dram_tensor("cw", [CONV_DIM, 4], F32, kind="ExternalInput")
    cb_in = nc.dram_tensor("cb", [128, CONV_DIM // 128], F32, kind="ExternalInput")
    dtb_in = nc.dram_tensor("dtb", [128, 1], F32, kind="ExternalInput")
    al_in = nc.dram_tensor("al", [128, 1], F32, kind="ExternalInput")
    m8_in = nc.dram_tensor("mask8t", [4, 4], F32, kind="ExternalInput")
    mv_in = nc.dram_tensor("maskv", [4, 1], F32, kind="ExternalInput")
    sel_in = nc.dram_tensor("e8sel", [4, 4 * 128], F32, kind="ExternalInput")
    d_in = nc.dram_tensor("dvec", [1, H], F32, kind="ExternalInput")
    nw_in = nc.dram_tensor("nw", [128, INTER // 128], F32, kind="ExternalInput")
    op_in = nc.dram_tensor("opT", [INTER, D], F32, kind="ExternalInput")
    zb_in = nc.dram_tensor("zbias", [1, D], F32, kind="ExternalInput")
    qw_in = nc.dram_tensor("qw", [D, D], I32, kind="ExternalInput")
    sc_in = nc.dram_tensor("sc", [D, 32], F32, kind="ExternalInput")
    zr_in = nc.dram_tensor("zr", [D, 32], F32, kind="ExternalInput")
    fb_in = nc.dram_tensor("fbias", [1, D], F32, kind="ExternalInput")
    wq_in = nc.dram_tensor("wq", [D, HEAD], F32, kind="ExternalInput")
    wk_in = nc.dram_tensor("wk", [D, HEAD], F32, kind="ExternalInput")
    wv_in = nc.dram_tensor("wv", [D, HEAD], F32, kind="ExternalInput")
    qb_in = nc.dram_tensor("qb", [128, HEAD // 128], F32, kind="ExternalInput")
    kb_in = nc.dram_tensor("kb", [128, HEAD // 128], F32, kind="ExternalInput")
    vb_in = nc.dram_tensor("vb", [1, HEAD], F32, kind="ExternalInput")
    wo_in = nc.dram_tensor("woT", [HEAD, D], F32, kind="ExternalInput")
    wob_in = nc.dram_tensor("wob", [1, D], F32, kind="ExternalInput")
    w1_in = nc.dram_tensor("w1T", [D, D], F32, kind="ExternalInput")
    db_in = nc.dram_tensor("dbias", [1, D], F32, kind="ExternalInput")
    lsi_in = nc.dram_tensor("lsi", [1, 4], I32, kind="ExternalInput")
    bid_in = nc.dram_tensor("bid", [1, 1], F32, kind="ExternalInput")
    lat_in = nc.dram_tensor("lat", [3, D, 16], F32, kind="ExternalInput")
    lbt_in = nc.dram_tensor("lbt", [3, 16, D], F32, kind="ExternalInput")
    out_o = nc.dram_tensor("out", [TOK, D], F32, kind="ExternalOutput")
    gate_d = nc.dram_tensor("gate_d", [TOK, INTER], BF16)
    hbc_d = nc.dram_tensor("hbc_d", [CONV_DIM, TOK + 3], BF16)
    dt_d = nc.dram_tensor("dt_d", [H, TOK], F32)
    hs_d = nc.dram_tensor("hs_d", [TOK, INTER], BF16)
    ccm_d = nc.dram_tensor("ccm_d", [G * Nd, TOK], BF16)
    acs_d = nc.dram_tensor("acs_d", [TOK, H], F32)
    dpre_d = nc.dram_tensor("dpre_d", [128, NCHUNK * H], F32)
    logd_d = nc.dram_tensor("logd_d", [1, H], F32)
    e_d = nc.dram_tensor("e_d", [128, INTER], F32)
    eag_d = nc.dram_tensor("eag_d", [4 * 128, INTER], F32)
    logdag_d = nc.dram_tensor("logdag_d", [4, H], F32)
    ydiag_d = nc.dram_tensor("ydiag_d", [TOK, INTER], F32)
    pfx_d = nc.dram_tensor("pfx_d", [TOK, INTER], BF16)
    xn_d = nc.dram_tensor("xn_d", [TOK, INTER], BF16)
    x1_d = nc.dram_tensor("x1_d", [TOK, D], F32)
    x2_d = nc.dram_tensor("x2_d", [TOK, D], F32)
    x3_d = nc.dram_tensor("x3_d", [TOK, D], F32)
    wdq_d = nc.dram_tensor("wdq_d", [D, D], BF16)
    qt_d = nc.dram_tensor("qt_d", [HEAD, TOK], BF16)
    kt_d = nc.dram_tensor("kt_d", [HEAD, TOK], BF16)
    v_d = nc.dram_tensor("v_d", [TOK, HEAD], BF16)
    ktag_d = nc.dram_tensor("ktag_d", [4 * HEAD, TOK], BF16)
    vag_d = nc.dram_tensor("vag_d", [4 * TOK, HEAD], BF16)
    GRP = [[0, 1, 2, 3], [4, 5, 6, 7]]

    p.push("a1")
    emit_A1(p, x_in, w_ip, gate_d, hbc_d, dt_d, xh_in=xh_in)
    p.barrier()
    p.push("a2")
    emit_A2(p, hbc_d, dt_d, cw_in, cb_in, dtb_in, al_in, hs_d, ccm_d, acs_d,
            dpre_d, logd_d, e_d, pfx_d, ydiag_d)
    p.barrier()
    p.push("ag1")
    cg1 = p.sem("cg")
    p.inc(nc.gpsimd.collective_compute("AllGather", ALU.bypass, GRP,
                                       [e_d.ap().opt()], [eag_d.ap().opt()]), cg1)
    p.inc(nc.gpsimd.collective_compute("AllGather", ALU.bypass, GRP,
                                       [logd_d.ap().opt()], [logdag_d.ap().opt()]), cg1)
    for e in [nc.tensor, nc.vector, nc.scalar, nc.gpsimd, nc.sync]:
        p.wait(e, cg1)
    p.barrier()
    p.push("a3a")
    emit_A3a(p, ydiag_d, hs_d, gate_d, ccm_d, pfx_d, acs_d, dpre_d, eag_d,
             logdag_d, m8_in, mv_in, sel_in, d_in, xn_d, NW=4)
    p.barrier()
    p.push("a3b")
    emit_mm(p, INTER, D, False, False, False, True,
            dict(x=xn_d, res=x_in, bias=zb_in, w=op_in, nw=nw_in, out=x1_d), OFW=256)
    p.barrier()
    p.push("bffn")
    emit_mm(p, D, D, True, True, False, False,
            dict(x=x1_d, res=x1_d, bias=fb_in, qw=qw_in, sc=sc_in, zr=zr_in,
                 wdq=wdq_d, out=x2_d))
    p.barrier()
    p.push("c1")
    emit_C1(p, x2_d, wq_in, wk_in, wv_in, qb_in, kb_in, vb_in, qt_d, kt_d, v_d)
    p.barrier()
    p.push("ag2")
    cg2 = p.sem("cg")
    p.inc(nc.gpsimd.collective_compute("AllGather", ALU.bypass, GRP,
                                       [kt_d.ap().opt()], [ktag_d.ap().opt()]), cg2)
    p.inc(nc.gpsimd.collective_compute("AllGather", ALU.bypass, GRP,
                                       [v_d.ap().opt()], [vag_d.ap().opt()]), cg2)
    for e in [nc.tensor, nc.vector, nc.scalar, nc.gpsimd, nc.sync]:
        p.wait(e, cg2)
    p.barrier()
    p.push("c2")
    emit_C2(p, qt_d, ktag_d, vag_d, wo_in, wob_in, x2_d, x3_d, kt_ranked=True)
    p.barrier()
    p.push("dffn")
    emit_mm(p, D, D, True, False, True, False,
            dict(x=x3_d, res=x3_d, bias=db_in, lsi=lsi_in, bid=bid_in,
                 lat=lat_in, lbt=lbt_in, w=w1_in, out=out_o))
    p.barrier()
    p.gstack.close()
    return p.nc


def kernel_fused(**inputs):
    f32 = np.float32
    hid = np.asarray(inputs["L0_attn_hidden_states"], f32).reshape(8192, D)
    wT = _c(np.asarray(inputs["L0_attn_in_proj_w"], f32).T)
    conv_w = np.asarray(inputs["L0_attn_conv_w"], f32).reshape(CONV_DIM, 4)
    conv_bT = _c(np.asarray(inputs["L0_attn_conv_b"], f32).reshape(80, 128).T)
    dt_bias = _c(np.asarray(inputs["L0_attn_dt_bias"], f32).reshape(128, 1))
    A_log = _c(np.asarray(inputs["L0_attn_A_log"], f32).reshape(128, 1))
    Dv = _c(np.asarray(inputs["L0_attn_D"], f32).reshape(1, H))
    nwT = _c(np.asarray(inputs["L0_attn_norm_weight"], f32).reshape(INTER // 128, 128).T)
    opT = _c(np.asarray(inputs["L0_attn_out_proj_w"], f32).T)
    qw = _c(np.asarray(inputs["L0_ffn_qweight"]).reshape(D, D).astype(np.int32))
    sc = _c(np.asarray(inputs["L0_ffn_scales"], f32).reshape(D, 32))
    zr = _c(np.asarray(inputs["L0_ffn_scaled_zeros"], f32).reshape(D, 32))
    fb = _c(np.asarray(inputs["L0_ffn_bias"], f32).reshape(1, D))
    wqk = np.asarray(inputs["L1_attn_Wqk_w"], f32)
    wqT = _c(wqk[0::2].T)
    wkT = _c(wqk[1::2].T)
    bqk = np.asarray(inputs["L1_attn_Wqk_b"], f32)
    qbT = _c(bqk[0::2].reshape(4, 128).T)
    kbT = _c(bqk[1::2].reshape(4, 128).T)
    wvT = _c(np.asarray(inputs["L1_attn_Wv_w"], f32).T)
    vb = _c(np.asarray(inputs["L1_attn_Wv_b"], f32).reshape(1, 512))
    woT = _c(np.asarray(inputs["L1_attn_Wo_w"], f32).T)
    wob = _c(np.asarray(inputs["L1_attn_Wo_b"], f32).reshape(1, D))
    w1T = _c(np.asarray(inputs["L1_ffn_weight"], f32).T)
    db = _c(np.asarray(inputs["L1_ffn_bias"], f32).reshape(1, D))
    latT = _c(np.asarray(inputs["L1_ffn_lora_A"], f32).transpose(0, 2, 1))
    lbtT = _c(np.asarray(inputs["L1_ffn_lora_B"], f32).transpose(0, 2, 1))
    lsi = _c(np.asarray(inputs["L1_ffn_lora_start_indices"]).reshape(1, 4).astype(np.int32))
    zeros_bias = np.zeros((1, D), f32)
    e8sel = np.zeros((4, 4 * 128), f32)
    for q in range(4):
        e8sel[q, q * 128:(q + 1) * 128] = 1.0
    maps = []
    for c in range(NCORES):
        pos = c % 4
        m8 = np.zeros((4, 4), f32)
        mv = np.zeros((4, 1), f32)
        for q in range(4):
            if q < pos:
                mv[q, 0] = 1.0
                for r in range(4):
                    if q < r < pos:
                        m8[r, q] = 1.0
        xh = np.zeros((3, D), f32) if pos == 0 else _c(hid[c * TOK - 3:c * TOK])
        maps.append(dict(x=_c(hid[c * TOK:(c + 1) * TOK]), xh=xh, w_ip=wT, cw=conv_w,
                         cb=conv_bT, dtb=dt_bias, al=A_log, mask8t=m8, maskv=mv,
                         e8sel=e8sel, dvec=Dv, nw=nwT, opT=opT, zbias=zeros_bias,
                         qw=qw, sc=sc, zr=zr, fbias=fb, wq=wqT, wk=wkT, wv=wvT,
                         qb=qbT, kb=kbT, vb=vb, woT=woT, wob=wob, w1T=w1T, dbias=db,
                         lsi=lsi, bid=np.full((1, 1), float(c // 4), f32),
                         lat=latT, lbt=lbtT))
    res = run_bass_kernel_spmd(build_fused(), maps, core_ids=list(range(NCORES)))
    out = np.concatenate([np.asarray(r["out"]) for r in res.results], axis=0)
    return out.reshape(2, 4096, D)
